# version 09
# baseline (speedup 1.0000x reference)
import numpy as np
from contextlib import ExitStack

import concourse.bass as bass
import concourse.mybir as mybir
from concourse.bass_utils import run_bass_kernel_spmd

F32 = mybir.dt.float32
BF16 = mybir.dt.bfloat16
AF = mybir.ActivationFunctionType
ALU = mybir.AluOpType
AX = mybir.AxisListType

NCORES = 8
N = 4096
D = 1024
DH = 64
HP = 8
NB = 8
NST = 32
SCALE = DH ** -0.5
MARGIN = 25.0
LN_EPS = 1e-5
RMS_EPS = float(np.finfo(np.float32).eps)

DEBUG = False

NSLOT = 4
_DMAQ = ("DMAS", "DMAG", "DMAA")
SEMS = ("PE", "ACT", "DVE", "POOL", "SP") + tuple(
    "%s%d" % (q, s) for q in _DMAQ for s in range(NSLOT))
ENG2CMP = {"tensor": "PE", "scalar": "ACT", "vector": "DVE",
           "gpsimd": "POOL", "sync": "SP"}
ENG2DMA = {"sync": "DMAS", "gpsimd": "DMAG", "scalar": "DMAA"}


class Op:
    __slots__ = ("eng", "fn", "deps", "dma", "tiny", "pre_drain",
                 "needs_inc", "marker", "sem", "idx", "slot_prev")


class Sched:
    def __init__(self):
        self.ops = []
        self.phase_idx = {}
        self._qcount = {}
        self._slot_last = {}

    def add(self, eng, fn, deps=(), dma=False, tiny=False, pre_drain=False):
        op = Op()
        op.eng = eng
        op.fn = fn
        flat = []
        for d in deps:
            if d is None:
                continue
            if isinstance(d, (list, tuple)):
                flat.extend(x for x in d if x is not None)
            else:
                flat.append(d)
        op.deps = flat
        op.dma = dma
        op.tiny = tiny
        op.pre_drain = pre_drain
        op.needs_inc = dma
        op.marker = None
        op.slot_prev = None
        op.idx = len(self.ops)
        for d in op.deps:
            assert d.idx < op.idx, "dep must precede op"
        if dma:
            base = ENG2DMA[eng]
            k = self._qcount.get(base, 0)
            self._qcount[base] = k + 1
            slot = k % NSLOT
            op.sem = "%s%d" % (base, slot)
            op.slot_prev = self._slot_last.get((base, slot))
            self._slot_last[(base, slot)] = op
        else:
            op.sem = ENG2CMP[eng]
        self.ops.append(op)
        return op

    def finalize(self, max_ops=None):
        if max_ops is None:
            max_ops = len(self.ops)
        self.max_ops = max_ops
        for op in self.ops[:max_ops]:
            for d in op.deps:
                if d.dma or d.eng != op.eng:
                    d.needs_inc = True
        counts = {s: 0 for s in SEMS}
        for op in self.ops[:max_ops]:
            if op.needs_inc:
                counts[op.sem] += 16 if op.dma else 1
                op.marker = counts[op.sem]

    def emit_engine(self, eng_name, eng, sems):
        wm = {}
        for op in self.ops[:self.max_ops]:
            if op.eng != eng_name:
                continue
            if op.dma and op.slot_prev is not None:
                p = op.slot_prev
                if p.marker > wm.get(p.sem, 0):
                    eng.wait_ge(sems[p.sem], p.marker)
                    wm[p.sem] = p.marker
            for d in op.deps:
                if d.eng == eng_name and not d.dma:
                    continue
                if d.marker > wm.get(d.sem, 0):
                    eng.wait_ge(sems[d.sem], d.marker)
                    wm[d.sem] = d.marker
            if op.pre_drain:
                eng.drain()
            inst = op.fn(eng)
            if op.needs_inc:
                if op.dma:
                    inst.then_inc(sems[op.sem], 16)
                elif op.tiny:
                    eng.drain().then_inc(sems[op.sem], 1)
                else:
                    inst.then_inc(sems[op.sem], 1)


def build_program(max_ops=None):
    nc = bass.Bass("TRN2", target_bir_lowering=False)

    def dram_in(name, shape):
        return nc.dram_tensor(name, shape, F32, kind="ExternalInput").ap()

    x_d = dram_in("x", [N, D])
    wqT_d = dram_in("wqT", [D, D])
    wkT_d = dram_in("wkT", [D, D])
    wvT_d = dram_in("wvT", [D, D])
    woT_d = dram_in("woT", [D, D])
    wcT_d = dram_in("wcT", [16 * D, D])
    railsY_d = dram_in("railsY", [128, 4096])
    railsZ_d = dram_in("railsZ", [128, 16384])
    qnwA_d = dram_in("qnwA", [1, 128]); qnwB_d = dram_in("qnwB", [1, 128])
    knwA_d = dram_in("knwA", [1, 128]); knwB_d = dram_in("knwB", [1, 128])
    indA_d = dram_in("indA", [128, 1]); indB_d = dram_in("indB", [128, 1])
    indAr_d = dram_in("indAr", [1, 128]); indBr_d = dram_in("indBr", [1, 128])
    ones1_d = dram_in("ones1", [1, 128])
    eye_d = dram_in("eye", [128, 128])

    y_d = nc.dram_tensor("y", [N, D], F32, kind="ExternalOutput").ap()

    ik = "ExternalOutput" if DEBUG else "Internal"
    qT_h = nc.dram_tensor("qT_h", [D, N], BF16, kind=ik).ap()
    kT_h = nc.dram_tensor("kT_h", [D, N], BF16, kind=ik).ap()
    v_h = nc.dram_tensor("v_h", [D, 32 * 130], BF16, kind=ik).ap()
    oT_h = nc.dram_tensor("oT_h", [D, N], BF16, kind=ik).ap()
    x2_h = nc.dram_tensor("x2_h", [N, D], F32, kind=ik).ap()
    h_h = nc.dram_tensor("h_h", [16 * D, N], BF16, kind=ik).ap()
    wcb_h = nc.dram_tensor("wcb_h", [16 * D, D], BF16, kind="Internal").ap()

    sched = Sched()
    S = sched.add
    guards = {}

    def g_get(key):
        return guards.get(key, [])

    def g_set(key, ops_):
        guards[key] = [o for o in ops_ if o is not None]

    with ExitStack() as ctx:
        def sb(name, shape, dt):
            return ctx.enter_context(nc.sbuf_tensor(name, shape, dt))

        BF1 = sb("BF1", [128, 32768], BF16)
        BF2 = sb("BF2", [128, 32768], BF16)
        BF3 = sb("BF3", [128, 16384], BF16)
        FA = sb("FA", [128, 8192], F32)
        SM = sb("SM", [128, 2048], F32)
        eye_s = sb("eye_s", [128, 128], F32)
        ones1_s = sb("ones1_s", [1, 128], F32)
        indA_s = sb("indA_s", [128, 1], F32)
        indB_s = sb("indB_s", [128, 1], F32)
        indAr_s = sb("indAr_s", [1, 128], F32)
        indBr_s = sb("indBr_s", [1, 128], F32)
        qnwA_s = sb("qnwA_s", [1, 128], F32)
        qnwB_s = sb("qnwB_s", [1, 128], F32)
        knwA_s = sb("knwA_s", [1, 128], F32)
        knwB_s = sb("knwB_s", [1, 128], F32)

        pb = [ctx.enter_context(nc.psum_tensor("pb%d" % i, [128, 512], F32))
              for i in range(8)]

        sems = {s: ctx.enter_context(nc.semaphore(s)) for s in SEMS}

        o_epsrms, o_epsln = 0, 1
        o_mean, o_var, o_lnv, o_rstd = 2, 4, 6, 8
        o_sum0, o_sum1, o_ssq0, o_ssq1, o_m2 = 10, 12, 14, 16, 18
        o_rstdA, o_rstdB = 32, 544
        o_sstat = 1056
        o_maxes = 1568
        o_maxhead = 1824
        o_craw, o_biasv = 1856, 1872
        o_biasT = 1888
        o_recipA, o_recipB = 1904, 1920
        o_recipA, o_recipB = o_rstdA, o_rstdB

        FA_x0, FA_x1 = 0, 1024
        FA_xn = (2048, 3072)
        FA_sq = 4096
        FA_sq2 = 5120
        FA_bc = 5632
        FA_att = 6144
        FA_x2 = 7168

        B3_W = 0
        B3_vaug = 2048
        B3_wcs = 10304
        B3_qf = 12352
        B3_ot = 13376
        B3_y0 = 12352
        B3_railsY = 2048
        B3_railsZ = 6400
        B3_wo = 2048
        B3_oTt = 0
        B3_wcb = 0
        B3_hg = 2048

        B1_qT = (0, 4096)
        B1_kT = (8192, 12288)
        B1_vau = (16384, 20544)

        def mm(dst, lhsT, rhs, start=True, stop=True, tile=None, deps=()):
            return S("tensor",
                     lambda t, d=dst, l=lhsT, r=rhs, st=start, sp=stop,
                     tp=tile: t.matmul(d, l, r, start=st, stop=sp,
                                       tile_position=tp),
                     deps=deps)

        def tr(dst, src, deps=()):
            return S("tensor",
                     lambda t, d=dst, s=src: t.transpose(d, s, eye_s[:]),
                     deps=deps)

        def act(out, in_, func, bias=None, scale=None, accum=None, deps=(),
                tiny=False):
            kw = {}
            if bias is not None:
                kw["bias"] = bias
            if scale is not None:
                kw["scale"] = scale
            if accum is not None:
                kw["accum_out"] = accum
            return S("scalar",
                     lambda s, o=out, i=in_, f=func, k=kw:
                     s.activation(o, i, f, **k),
                     deps=deps, tiny=tiny, pre_drain=tiny)

        def acopy(out, in_, deps=(), tiny=False):
            return S("scalar", lambda s, o=out, i=in_: s.copy(o, i),
                     deps=deps, tiny=tiny, pre_drain=tiny)

        def dve(fn, deps=(), tiny=False):
            return S("vector", fn, deps=deps, tiny=tiny, pre_drain=tiny)

        def dma(queue, dst, src, deps=()):
            return S(queue, lambda e, d=dst, s=src: e.dma_start(d, s),
                     deps=deps, dma=True)

        ld_eye = dma("gpsimd", eye_s[:], eye_d)
        ld_ones1 = dma("sync", ones1_s[:], ones1_d)
        ld_indA = dma("sync", indA_s[:], indA_d)
        ld_indB = dma("sync", indB_s[:], indB_d)
        ld_indAr = dma("sync", indAr_s[:], indAr_d)
        ld_indBr = dma("sync", indBr_s[:], indBr_d)
        ld_qnw = [dma("sync", qnwA_s[:], qnwA_d), dma("sync", qnwB_s[:], qnwB_d)]
        ld_knw = [dma("sync", knwA_s[:], knwA_d), dma("sync", knwB_s[:], knwB_d)]
        ms1 = S("vector", lambda v: v.memset(SM[:, o_epsrms:o_epsrms + 1],
                                             RMS_EPS), tiny=True)
        ms2 = S("vector", lambda v: v.memset(SM[:, o_epsln:o_epsln + 1],
                                             LN_EPS), tiny=True)
        vaug = BF3[:, B3_vaug:B3_vaug + 4160]
        vaug_ones = []
        for jc in range(32):
            for extra in (64, 129):
                vaug_ones.append(
                    S("gpsimd", lambda g, c=B3_vaug + jc * 130 + extra:
                      g.memset(BF3[:, c:c + 1], 1.0), tiny=True))

        xt_tile = [FA[:, FA_x0:FA_x0 + 1024], FA[:, FA_x1:FA_x1 + 1024]]

        def ln_stats(src0, src1, slot, deps0, deps1, from_psum=False):
            mean = SM[:, o_mean + slot:o_mean + slot + 1]
            var = SM[:, o_var + slot:o_var + slot + 1]
            lnv = SM[:, o_lnv + slot:o_lnv + slot + 1]
            rstd = SM[:, o_rstd + slot:o_rstd + slot + 1]
            s0 = SM[:, o_sum0 + slot:o_sum0 + slot + 1]
            s1 = SM[:, o_sum1 + slot:o_sum1 + slot + 1]
            q0 = SM[:, o_ssq0 + slot:o_ssq0 + slot + 1]
            q1 = SM[:, o_ssq1 + slot:o_ssq1 + slot + 1]
            m2 = SM[:, o_m2 + slot:o_m2 + slot + 1]
            sq = FA[:, FA_sq:FA_sq + 1024]
            readers = []
            if src1 is None:
                r1 = dve(lambda v, o=s0, i=src0:
                         v.tensor_reduce(o, i, axis=AX.X, op=ALU.add),
                         deps=deps0, tiny=True)
                a1 = act(sq[:, 0:1024], src0, AF.Square, accum=q0,
                         deps=deps0 + ([r1] if from_psum else []), tiny=True)
                c1, c2 = r1, a1
                readers = [r1, a1]
            else:
                r1 = dve(lambda v, o=s0, i=src0:
                         v.tensor_reduce(o, i, axis=AX.X, op=ALU.add),
                         deps=deps0, tiny=True)
                r2 = dve(lambda v, o=s1, i=src1:
                         v.tensor_reduce(o, i, axis=AX.X, op=ALU.add),
                         deps=deps1, tiny=True)
                a1 = act(sq[:, 0:512], src0, AF.Square, accum=q0,
                         deps=deps0 + ([r1] if from_psum else []), tiny=True)
                a2 = act(sq[:, 512:1024], src1, AF.Square, accum=q1,
                         deps=deps1 + ([r2] if from_psum else []), tiny=True)
                c1 = dve(lambda v, o=s0, a=s0, b=s1: v.tensor_add(o, a, b),
                         deps=[r1, r2], tiny=True)
                c2 = dve(lambda v, o=q0, a=q0, b=q1: v.tensor_add(o, a, b),
                         deps=[a1, a2], tiny=True)
                readers = [r1, r2, a1, a2]
            t1 = dve(lambda v, o=mean, i=s0:
                     v.tensor_scalar(o, i, 1.0 / D, None, ALU.mult),
                     deps=[c1], tiny=True)
            t2 = dve(lambda v, o=var, i=q0:
                     v.tensor_scalar(o, i, 1.0 / D, None, ALU.mult),
                     deps=[c2], tiny=True)
            t3 = dve(lambda v, o=m2, a=mean, b=mean: v.tensor_mul(o, a, b),
                     deps=[t1], tiny=True)
            t4 = dve(lambda v, o=var, a=var, b=m2: v.tensor_sub(o, a, b),
                     deps=[t2, t3], tiny=True)
            a3 = act(lnv, var, AF.Ln, bias=SM[:, o_epsln:o_epsln + 1],
                     deps=[t4, ms2], tiny=True)
            a4 = act(rstd, lnv, AF.Exp, scale=-0.5, deps=[a3], tiny=True)
            return mean, rstd, a4, readers

        tr_slot = {}
        p1_evac = []
        for st in range(NST):
            par = st % 2
            ld = dma("sync", xt_tile[par], x_d[st * 128:(st + 1) * 128, :],
                     deps=g_get(("xt", par)))
            mean, rstd, statop, readers = ln_stats(
                xt_tile[par][:, 0:1024], None, par, [ld], None)
            xn = FA[:, FA_xn[par]:FA_xn[par] + 1024]
            nrm = dve(lambda v, o=xn, i=xt_tile[par], m=mean, r=rstd:
                      v.tensor_scalar(o, i, m, r, ALU.subtract, ALU.mult),
                      deps=[statop] + g_get(("xn", par)))
            g_set(("xt", par), readers + [nrm])
            trs = []
            for c in range(8):
                slot = (st * 8 + c) % 2
                bank = pb[slot]
                t = tr(bank[:, 0:128], xn[:, c * 128:c * 128 + 128],
                       deps=[nrm, ld_eye, tr_slot.get(slot)])
                ev = acopy(BF1[:, c * 4096 + st * 128:
                               c * 4096 + st * 128 + 128],
                           bank[:, 0:128], deps=[t])
                tr_slot[slot] = ev
                trs.append(t)
                p1_evac.append(ev)
            g_set(("xn", par), trs)
        p1_done = p1_evac[-1]
        sched.phase_idx["p1"] = len(sched.ops)

        wt_tile = [BF3[:, B3_W:B3_W + 1024], BF3[:, B3_W + 1024:B3_W + 2048]]
        qf_tile = [BF3[:, B3_qf:B3_qf + 512], BF3[:, B3_qf + 512:B3_qf + 1024]]
        projb = (2, 3)
        statb = (4, 5)
        bcb = 6
        wsrc = {"q": wqT_d, "k": wkT_d, "v": wvT_d}
        whb = {"q": qT_h, "k": kT_h}

        def wslice_ap(wd, hp):
            return wd.rearrange("(kc p) c -> p kc c", p=128)[
                :, :, hp * 128:(hp + 1) * 128]

        def w3d(tile):
            return tile.rearrange("p (kc c) -> p kc c", kc=8)

        maxes_written = []
        last_vdma = None
        for hp in range(HP):
            for dsti, dst in enumerate(("q", "k")):
                wpar = (hp * 2 + dsti) % 2
                ldw = dma("gpsimd", w3d(wt_tile[wpar]),
                          wslice_ap(wsrc[dst], hp),
                          deps=g_get(("wt", wpar)))
                wreaders = []
                for ib in range(NB):
                    ppar = ib % 2
                    bank = pb[projb[ppar]]
                    mmop = None
                    for kc in range(8):
                        mmop = mm(bank[:, :],
                                  wt_tile[wpar][:, kc * 128:kc * 128 + 128],
                                  BF1[:, kc * 4096 + ib * 512:
                                      kc * 4096 + ib * 512 + 512],
                                  start=(kc == 0), stop=(kc == 7),
                                  deps=([ldw, p1_done] +
                                        g_get(("projb", ppar))
                                        if kc == 0 else ()))
                    wreaders.append(mmop)
                    sq = FA[:, FA_sq:FA_sq + 512]
                    sqop = act(sq, bank[:, :], AF.Square,
                               deps=[mmop] + g_get("sqr"))
                    stA = mm(pb[statb[0]][0:1, :], indA_s[:], sq,
                             deps=[sqop, ld_indA] + g_get(("stb", 0)))
                    stB = mm(pb[statb[1]][0:1, :], indB_s[:], sq,
                             deps=[sqop, ld_indB] + g_get(("stb", 1)))
                    g_set("sqr", [stA, stB])
                    rstdA = SM[0:1, o_rstdA:o_rstdA + 512]
                    rstdB = SM[0:1, o_rstdB:o_rstdB + 512]
                    lA = act(rstdA, pb[statb[0]][0:1, :], AF.Ln,
                             bias=SM[0:1, o_epsrms:o_epsrms + 1],
                             scale=1.0 / DH, deps=[stA, ms1])
                    eA = act(rstdA, rstdA, AF.Exp, scale=-0.5, deps=[lA],
                             tiny=True)
                    lB = act(rstdB, pb[statb[1]][0:1, :], AF.Ln,
                             bias=SM[0:1, o_epsrms:o_epsrms + 1],
                             scale=1.0 / DH, deps=[stB, ms1])
                    eB = act(rstdB, rstdB, AF.Exp, scale=-0.5, deps=[lB],
                             tiny=True)
                    g_set(("stb", 0), [lA])
                    g_set(("stb", 1), [lB])
                    wA = qnwA_s if dst == "q" else knwA_s
                    wB = qnwB_s if dst == "q" else knwB_s
                    bc1 = mm(pb[bcb][:, :], wA[:], rstdA, start=True,
                             stop=False,
                             deps=[eA] + ld_qnw + ld_knw + g_get("bcb"))
                    bc2 = mm(pb[bcb][:, :], wB[:], rstdB, start=False,
                             stop=True, deps=[eB])
                    bce = act(FA[:, FA_bc:FA_bc + 512], pb[bcb][:, :],
                              AF.Identity, deps=[bc2] + g_get("fabc"))
                    g_set("bcb", [bce])
                    qfin = qf_tile[ppar]
                    qop = dve(lambda v, o=qfin, a=bank[:, :],
                              b=FA[:, FA_bc:FA_bc + 512]:
                              v.tensor_tensor(o, a, b, op=ALU.mult),
                              deps=[bce] + g_get(("qf", ppar)))
                    guards["fabc"] = [qop]
                    g_set(("projb", ppar), [qop, sqop])
                    st_dma = dma("sync",
                                 whb[dst][hp * 128:(hp + 1) * 128,
                                          ib * 512:(ib + 1) * 512],
                                 qfin, deps=[qop])
                    sq2 = FA[:, FA_sq2:FA_sq2 + 512]
                    sqop2 = act(sq2, qfin, AF.Square,
                                deps=[qop] + g_get("sq2r"))
                    g_set(("qf", ppar), [st_dma, sqop2])
                    fsA = mm(pb[statb[0]][0:1, :], indA_s[:], sq2,
                             deps=[sqop2] + g_get(("stb", 0)))
                    fsB = mm(pb[statb[1]][0:1, :], indB_s[:], sq2,
                             deps=[sqop2] + g_get(("stb", 1)))
                    g_set("sq2r", [fsA, fsB])
                    for hd, fs, bk in ((0, fsA, statb[0]), (1, fsB, statb[1])):
                        slot = (dsti * 16 + hp * 2 + hd) * 8 + ib
                        mx = dve(lambda v,
                                 o=SM[0:1, o_maxes + slot:o_maxes + slot + 1],
                                 i=pb[bk][0:1, :]:
                                 v.tensor_reduce(o, i, axis=AX.X, op=ALU.max),
                                 deps=[fs], tiny=True)
                        maxes_written.append(mx)
                        g_set(("stb", hd), [mx])
                g_set(("wt", wpar), wreaders)

            wpar = hp % 2
            ldw = dma("gpsimd", w3d(wt_tile[wpar]), wslice_ap(wvT_d, hp),
                      deps=g_get(("wt", wpar)))
            wreaders = []
            vops = []
            for ib in range(NB):
                ppar = ib % 2
                bank = pb[projb[ppar]]
                mmop = None
                for kc in range(8):
                    mmop = mm(bank[:, :],
                              wt_tile[wpar][:, kc * 128:kc * 128 + 128],
                              BF1[:, kc * 4096 + ib * 512:
                                  kc * 4096 + ib * 512 + 512],
                              start=(kc == 0), stop=(kc == 7),
                              deps=([ldw, p1_done] + g_get(("projb", ppar))
                                    if kc == 0 else ()))
                wreaders.append(mmop)
                vt = FA[:, FA_att:FA_att + 512]
                ev = acopy(vt, bank[:, :], deps=[mmop] + g_get("vt"))
                g_set(("projb", ppar), [ev])
                trs = []
                for c in range(4):
                    jc = ib * 4 + c
                    slot = jc % 2
                    bank2 = pb[slot]
                    t = tr(bank2[:, 0:128],
                           vt[:, c * 128:c * 128 + 128],
                           deps=[ev, ld_eye, tr_slot.get(slot)] +
                                g_get("vaug_dma"))
                    e1 = acopy(vaug[:, jc * 130:jc * 130 + 64],
                               bank2[:, 0:64], deps=[t])
                    e2 = acopy(vaug[:, jc * 130 + 65:jc * 130 + 129],
                               bank2[:, 64:128], deps=[t])
                    tr_slot[slot] = e2
                    trs.append(t)
                    vops.extend([e1, e2])
                g_set("vt", trs)
            vd = dma("sync", v_h[hp * 128:(hp + 1) * 128, :], vaug,
                     deps=vops + vaug_ones)
            g_set("vaug_dma", [vd])
            last_vdma = vd
            g_set(("wt", wpar), wreaders + [vd])

        mh_ops = []
        for i in range(32):
            op = dve(lambda v, o=SM[0:1, o_maxhead + i:o_maxhead + i + 1],
                     s=SM[0:1, o_maxes + i * 8:o_maxes + i * 8 + 8]:
                     v.tensor_reduce(o, s, axis=AX.X, op=ALU.max),
                     deps=[], tiny=True)
            mh_ops.append(op)
        craw = SM[0:1, o_craw:o_craw + 16]
        cop = dve(lambda v, o=craw, a=SM[0:1, o_maxhead:o_maxhead + 16],
                  b=SM[0:1, o_maxhead + 16:o_maxhead + 32]:
                  v.tensor_mul(o, a, b), deps=[], tiny=True)
        csq = act(craw, craw, AF.Sqrt, deps=[cop], tiny=True)
        biasv = SM[0:1, o_biasv:o_biasv + 16]
        bop = dve(lambda v, o=biasv, i=craw:
                  v.tensor_scalar(o, i, -SCALE, MARGIN, ALU.mult, ALU.add),
                  deps=[csq], tiny=True)
        bcm = mm(pb[bcb][:, 0:16], ones1_s[:], biasv,
                 deps=[bop, ld_ones1] + g_get("bcb"))
        biasT = SM[:, o_biasT:o_biasT + 16]
        bias_ready = acopy(biasT, pb[bcb][:, 0:16], deps=[bcm], tiny=True)
        sched.phase_idx["p15"] = len(sched.ops)


        qT_buf = [BF1[:, B1_qT[0]:B1_qT[0] + 4096],
                  BF1[:, B1_qT[1]:B1_qT[1] + 4096]]
        kT_buf = [BF1[:, B1_kT[0]:B1_kT[0] + 4096],
                  BF1[:, B1_kT[1]:B1_kT[1] + 4096]]
        va_buf = [BF1[:, B1_vau[0]:B1_vau[0] + 4160],
                  BF1[:, B1_vau[1]:B1_vau[1] + 4160]]
        Ablk = [BF2[:, 0:16384], BF2[:, 16384:32768]]
        ot_tile = [BF3[:, B3_ot:B3_ot + 512], BF3[:, B3_ot + 512:B3_ot + 1024]]
        SCB = (0, 1, 2)
        AVB = {0: (3, 4), 1: (5, 6)}
        RCB = 7
        for s in range(3):
            g_set(("scb3", s), [bias_ready])
        for hd in range(2):
            for p in range(2):
                g_set(("avb", hd, p), [bias_ready])
        g_set("rcb", [bias_ready])
        g_set(("qkv", 0), [last_vdma])
        g_set(("qkv", 1), [last_vdma])
        Ause = {}
        qkv_loaded = [None, None]

        def load_hp(hp):
            par = hp % 2
            dep = g_get(("qkv", par))
            l1 = dma("sync", qT_buf[par], qT_h[hp * 128:(hp + 1) * 128, :],
                     deps=dep)
            l2 = dma("sync", kT_buf[par], kT_h[hp * 128:(hp + 1) * 128, :],
                     deps=dep)
            l3 = dma("sync", va_buf[par], v_h[hp * 128:(hp + 1) * 128, :],
                     deps=dep)
            qkv_loaded[par] = [l1, l2, l3]

        g_set("p2bc", g_get("fabc"))
        last_od = [None, None]
        step = 0
        load_hp(0)
        for hp in range(HP):
            par = hp % 2
            if hp + 1 < HP:
                load_hp(hp + 1)
            loads = qkv_loaded[par]
            av_done = {}
            pend = []
            eop_of = {}

            def emit_tail(ib):
                apar = ib % 2
                otp = ib % 2
                rsb = [SM[0:1, o_recipA:o_recipA + 512],
                       SM[0:1, o_recipB:o_recipB + 512]]
                cc = []
                for hd in range(2):
                    cc.append(acopy(rsb[hd], pb[AVB[hd][apar]][64:65, :],
                                    deps=[av_done[(ib, hd)]] +
                                         g_get(("rrow", hd))))
                bc1 = mm(pb[RCB][:, :], indAr_s[:], rsb[0], start=True,
                         stop=False,
                         deps=[cc[0], ld_indAr, ld_indBr] + g_get("rcb"))
                bc2 = mm(pb[RCB][:, :], indBr_s[:], rsb[1], start=False,
                         stop=True, deps=[cc[1]])
                g_set(("rrow", 0), [bc1])
                g_set(("rrow", 1), [bc2])
                dbc = FA[:, FA_bc:FA_bc + 512]
                bce = act(dbc, pb[RCB][:, :], AF.Identity,
                          deps=[bc2] + g_get("p2bc"))
                g_set("rcb", [bce])
                rcop = dve(lambda v, o=dbc, i=dbc: v.reciprocal(o, i),
                           deps=[bce])
                oops = []
                for hd in range(2):
                    oop = dve(lambda v,
                              o=ot_tile[otp][hd * 64:hd * 64 + 64, :],
                              a=pb[AVB[hd][apar]][0:64, :],
                              b=dbc[hd * 64:hd * 64 + 64, :]:
                              v.tensor_tensor(o, a, b, op=ALU.mult),
                              deps=[rcop] + g_get(("ot", otp)))
                    g_set(("avb", hd, apar), [oop, cc[hd]])
                    oops.append(oop)
                g_set("p2bc", oops + [rcop])
                od = dma("sync", oT_h[hp * 128:(hp + 1) * 128,
                                      ib * 512:(ib + 1) * 512],
                         ot_tile[otp], deps=oops)
                g_set(("ot", otp), [od])
                last_od[otp] = od

            def issue_avop(ib, jc, hd):
                apar = ib % 2
                vcol = jc * 130 + hd * 65
                avop = mm(pb[AVB[hd][apar]][0:65, :],
                          va_buf[par][:, vcol:vcol + 65],
                          Ablk[hd][:, jc * 512:jc * 512 + 512],
                          start=(jc == 0), stop=(jc == 31),
                          deps=[eop_of[(ib, jc, hd)], loads[2]] +
                               (g_get(("avb", hd, apar)) if jc == 0 else []))
                Ause[(hd, jc)] = avop
                if jc == 31:
                    av_done[(ib, hd)] = avop
                    if hd == 1:
                        emit_tail(ib)
                return avop

            for ib in range(NB):
                for jc in range(32):
                    for hd in range(2):
                        s = step % 3
                        step += 1
                        sb_ = pb[SCB[s]]
                        prow = hd * 64
                        scop = mm(sb_[:, :],
                                  kT_buf[par][prow:prow + 64,
                                              jc * 128:jc * 128 + 128],
                                  qT_buf[par][prow:prow + 64,
                                              ib * 512:ib * 512 + 512],
                                  tile=(prow, 0),
                                  deps=[loads[0], loads[1]] +
                                       g_get(("scb3", s)))
                        adst = Ablk[hd][:, jc * 512:jc * 512 + 512]
                        eop = act(adst, sb_[:, :], AF.Exp,
                                  bias=biasT[:, hp * 2 + hd:hp * 2 + hd + 1],
                                  scale=SCALE,
                                  deps=[scop, bias_ready,
                                        Ause.get((hd, jc))])
                        g_set(("scb3", s), [eop])
                        eop_of[(ib, jc, hd)] = eop
                        pend.append((ib, jc, hd))
                        if len(pend) >= 6:
                            for _ in range(3):
                                issue_avop(*pend.pop(0))
            while pend:
                issue_avop(*pend.pop(0))
            g_set(("qkv", par), [av_done[(NB - 1, 0)], av_done[(NB - 1, 1)]])
        p2_last_od = [d for d in last_od if d is not None]
        g_set(("scb", 1, 0), p2_last_od)
        g_set(("scb", 1, 1), p2_last_od)
        sched.phase_idx["p2"] = len(sched.ops)

        oTt = [BF3[:, B3_oTt:B3_oTt + 1024],
               BF3[:, B3_oTt + 1024:B3_oTt + 2048]]
        wo_sb = BF3[:, B3_wo:B3_wo + 8192]
        ldwo = []
        for c in range(8):
            deps = []
            if c == 0:
                deps = (g_get(("qkv", 0)) + g_get(("qkv", 1)) +
                        p2_last_od + g_get(("ot", 0)) + g_get(("ot", 1)) +
                        g_get(("wcs", 0)) + g_get(("wcs", 1)) +
                        g_get(("wt", 0)) + g_get(("wt", 1)) +
                        g_get("vaug_dma"))
            ldwo.append(dma("gpsimd", wo_sb[:, c * 1024:(c + 1) * 1024],
                            woT_d[c * 128:(c + 1) * 128, :], deps=deps))
        wob = (0, 1)
        g_set(("wob", 0), [bias_ready] + p2_last_od)
        g_set(("wob", 1), [bias_ready] + p2_last_od)
        oT_src = oT_h.rearrange("(kc p) t -> p kc t", p=128)
        ln2_evac = []
        x2st_done = []
        tr3_slot = {}
        pend_trs = None
        for st in range(NST):
            par = st % 2
            lo = dma("sync", w3d(oTt[par]),
                     oT_src[:, :, st * 128:(st + 1) * 128],
                     deps=g_get(("oTt", par)) + p2_last_od +
                          (g_get(("wt", 0)) + g_get(("wt", 1))
                           if st < 2 else []))
            lx = dma("sync", xt_tile[par], x_d[st * 128:(st + 1) * 128, :],
                     deps=g_get(("xt", par)))
            mops = []
            for nh in range(2):
                mmop = None
                for kc in range(8):
                    mmop = mm(pb[wob[nh]][:, :],
                              oTt[par][:, kc * 128:kc * 128 + 128],
                              wo_sb[:, kc * 1024 + nh * 512:
                                    kc * 1024 + nh * 512 + 512],
                              start=(kc == 0), stop=(kc == 7),
                              deps=([ldwo[kc], lo] + g_get(("wob", nh))
                                    if kc == 0 else [ldwo[kc]]))
                mops.append(mmop)
            g_set(("oTt", par), mops)
            attb = (FA[:, FA_att:FA_att + 1024] if par == 0
                    else FA[:, FA_sq2:FA_sq2 + 1024])
            e0 = acopy(attb[:, 0:512], pb[wob[0]][:, :],
                       deps=[mops[0]] + g_get(("p3evt", par)))
            e1 = acopy(attb[:, 512:1024], pb[wob[1]][:, :],
                       deps=[mops[1]] + g_get(("p3evt", par)))
            g_set(("wob", 0), [e0])
            g_set(("wob", 1), [e1])
            mean, rstd, statop, readers = ln_stats(
                attb[:, 0:1024], None, par, [e0, e1], None)
            nrm = dve(lambda v, o=attb, i=attb, m=mean, r=rstd:
                      v.tensor_scalar(o, i, m, r, ALU.subtract, ALU.mult),
                      deps=[statop])
            x2t = FA[:, FA_x2:FA_x2 + 1024]
            xa = dve(lambda v, o=x2t, a=xt_tile[par], b=attb:
                     v.tensor_add(o, a, b),
                     deps=[lx, nrm] + g_get("x2t"))
            g_set(("p3evt", par), readers + [nrm, xa])
            g_set(("xt", par), [xa])
            sx2 = dma("scalar", x2_h[st * 128:(st + 1) * 128, :], x2t,
                      deps=[xa])
            x2st_done.append(sx2)
            mean2, rstd2, statop2, readers2 = ln_stats(
                x2t[:, 0:1024], None, par, [xa], None)
            ln2f = FA[:, FA_xn[par]:FA_xn[par] + 1024]
            l2 = dve(lambda v, o=ln2f, i=x2t, m=mean2, r=rstd2:
                     v.tensor_scalar(o, i, m, r, ALU.subtract, ALU.mult),
                     deps=[statop2] + g_get(("xn", par)))
            g_set("x2t", [sx2] + readers2 + [l2])

            def emit_trs(st_, l2_, ln2f_):
                par_ = st_ % 2
                trs = []
                for c in range(8):
                    slot = (st_ * 8 + c) % 2
                    bank = pb[slot + 2]
                    prior = tr3_slot.get(slot)
                    extra = ([] if prior is not None else
                             g_get(("scb", 1, 0)) + g_get(("scb", 1, 1)))
                    t = tr(bank[:, 0:128], ln2f_[:, c * 128:c * 128 + 128],
                           deps=[l2_, prior] + extra)
                    ev = acopy(BF1[:, c * 4096 + st_ * 128:
                                   c * 4096 + st_ * 128 + 128],
                               bank[:, 0:128], deps=[t])
                    tr3_slot[slot] = ev
                    trs.append(t)
                    ln2_evac.append(ev)
                g_set(("xn", par_), trs)

            if pend_trs is not None:
                emit_trs(*pend_trs)
            pend_trs = (st, l2, ln2f)
        emit_trs(*pend_trs)
        p3_done = ln2_evac[-1]
        sched.phase_idx["p3"] = len(sched.ops)

        railsY_sb = BF3[:, B3_railsY:B3_railsY + 4096]
        ldry = dma("gpsimd", railsY_sb, railsY_d, deps=[p3_done, ldwo[-1]])
        rz_tile = [BF3[:, B3_railsZ:B3_railsZ + 1024],
                   BF3[:, B3_railsZ + 1024:B3_railsZ + 2048]]
        y_tile = [BF3[:, B3_y0:B3_y0 + 512], BF3[:, B3_y0 + 512:B3_y0 + 1024]]
        yb = (0, 1)
        zb = (2, 3)
        g_set(("yb", 0), [p3_done]); g_set(("yb", 1), [p3_done])
        g_set(("zb", 0), [p3_done]); g_set(("zb", 1), [p3_done])
        g_set(("ytl", 0), [p3_done]); g_set(("ytl", 1), [p3_done])
        g_set(("qfdead", 0), g_get(("qf", 0)))
        hstore = {}
        slab_last = None
        for rb in range(16):
            hh, qr = rb // 4, rb % 4
            rpar = rb % 2
            lz = dma("gpsimd", rz_tile[rpar],
                     railsZ_d[:, rb * 1024:(rb + 1) * 1024],
                     deps=g_get(("rz", rpar)) + [p3_done, ldwo[-1]])
            gel_last = [None] * 8
            zreads = {0: [], 1: []}
            for tb4 in range(NB):
                yev = []
                for kr in range(2):
                    kcx = 2 * qr + kr
                    ym = mm(pb[yb[kr]][:, :],
                            railsY_sb[:, kcx * 512 + hh * 128:
                                      kcx * 512 + hh * 128 + 128],
                            BF1[:, kcx * 4096 + tb4 * 512:
                                kcx * 4096 + tb4 * 512 + 512],
                            deps=[ldry, p3_done] + g_get(("yb", kr)))
                    ye = acopy(y_tile[kr], pb[yb[kr]][:, :],
                               deps=[ym] + g_get(("ytl", kr)) +
                                    (g_get(("qf", 0)) + g_get(("qf", 1))
                                     if rb == 0 and tb4 == 0 else []))
                    g_set(("yb", kr), [ye])
                    yev.append(ye)
                zr = {0: [], 1: []}
                for fc in range(8):
                    kr, m = fc // 4, fc % 4
                    zpar = fc % 2
                    zm = mm(pb[zb[zpar]][:, :],
                            rz_tile[rpar][:, fc * 128:fc * 128 + 128],
                            y_tile[kr],
                            deps=[lz, yev[kr]] + g_get(("zb", zpar)))
                    zr[kr].append(zm)
                    gl = act(BF2[:, fc * 4096 + tb4 * 512:
                                 fc * 4096 + tb4 * 512 + 512],
                             pb[zb[zpar]][:, :], AF.Gelu,
                             deps=[zm] + g_get(("slab", fc)) +
                                  ([Ause.get((fc // 4, 31))]
                                   if rb == 0 and tb4 == 0 else []))
                    g_set(("zb", zpar), [gl])
                    gel_last[fc] = gl
                for kr in range(2):
                    g_set(("ytl", kr), zr[kr])
                    zreads[kr].extend(zr[kr])
            sds = []
            for fc in range(8):
                fkc = hh * 32 + qr * 8 + fc
                sd = dma("sync", h_h[fkc * 128:(fkc + 1) * 128, :],
                         BF2[:, fc * 4096:(fc + 1) * 4096],
                         deps=[gel_last[fc]])
                g_set(("slab", fc), [sd])
                hstore[fkc] = sd
                sds.append(sd)
            g_set(("rz", rpar), zreads[0] + zreads[1])
            slab_last = sds[-1]

        sched.phase_idx["p4a"] = len(sched.ops)
        wcb_tile = [BF3[:, B3_wcb:B3_wcb + 1024],
                    BF3[:, B3_wcb + 1024:B3_wcb + 2048]]
        hg_tile = [BF3[:, B3_hg:B3_hg + 2048],
                   BF3[:, B3_hg + 2048:B3_hg + 4096]]
        h_src = h_h.rearrange("(kg g p) t -> kg p g t", p=128, g=4)
        for b in range(8):
            g_set(("accb", b), [slab_last])
        g_set(("hg", 0), [slab_last]); g_set(("hg", 1), [slab_last])
        g_set(("wcbt", 0), [slab_last]); g_set(("wcbt", 1), [slab_last])
        y_dmas = []
        for stg in range(8):
            acc_stops = []
            for kg in range(32):
                gpar = kg % 2
                lh = dma("sync",
                         hg_tile[gpar].rearrange("p (g t) -> p g t", g=4),
                         h_src[kg, :, :, stg * 512:(stg + 1) * 512],
                         deps=g_get(("hg", gpar)) +
                              [hstore[kg * 4 + g] for g in range(4)])
                hg_readers = []
                for kcl in range(4):
                    kc = kg * 4 + kcl
                    wpar = kc % 2
                    lw = dma("gpsimd", wcb_tile[wpar],
                             wcT_d[kc * 128:(kc + 1) * 128, :],
                             deps=g_get(("wcbt", wpar)))
                    wreaders = []
                    for st4 in range(4):
                        for nh in range(2):
                            bnk = st4 * 2 + nh
                            op = mm(pb[bnk][:, :],
                                    hg_tile[gpar][:,
                                                  kcl * 512 + st4 * 128:
                                                  kcl * 512 + st4 * 128 + 128],
                                    wcb_tile[wpar][:, nh * 512:nh * 512 + 512],
                                    start=(kc == 0), stop=(kc == 127),
                                    deps=[lh, lw] +
                                         (g_get(("accb", bnk))
                                          if kc == 0 else []))
                            wreaders.append(op)
                            hg_readers.append(op)
                            if kc == 127:
                                acc_stops.append(op)
                    g_set(("wcbt", wpar), wreaders)
                g_set(("hg", gpar), hg_readers)
            for st4 in range(4):
                st = stg * 4 + st4
                par = st % 2
                m0, m1 = acc_stops[st4 * 2], acc_stops[st4 * 2 + 1]
                evt = FA[:, FA_xn[par]:FA_xn[par] + 1024]
                e0 = acopy(evt[:, 0:512], pb[st4 * 2][:, :],
                           deps=[m0] + g_get(("evt", par)))
                e1 = acopy(evt[:, 512:1024], pb[st4 * 2 + 1][:, :],
                           deps=[m1] + g_get(("evt", par)))
                g_set(("accb", st4 * 2), [e0])
                g_set(("accb", st4 * 2 + 1), [e1])
                lx2 = dma("scalar", xt_tile[par],
                          x2_h[st * 128:(st + 1) * 128, :],
                          deps=g_get(("xt", par)) + [x2st_done[st]])
                mean, rstd, statop, readers = ln_stats(
                    evt[:, 0:1024], None, par, [e0, e1], None)
                att = FA[:, FA_att:FA_att + 1024]
                n0 = dve(lambda v, o=att, i=evt, m=mean, r=rstd:
                         v.tensor_scalar(o, i, m, r, ALU.subtract, ALU.mult),
                         deps=[statop] + g_get("p4att"))
                g_set(("evt", par), readers + [n0])
                yt = FA[:, FA_x2:FA_x2 + 1024]
                ya = dve(lambda v, o=yt, a=xt_tile[par], b=att:
                         v.tensor_add(o, a, b),
                         deps=[lx2, n0] + g_get("x2t"))
                g_set("p4att", [ya])
                g_set(("xt", par), [ya])
                yd = dma("scalar", y_d[st * 128:(st + 1) * 128, :], yt,
                         deps=[ya])
                g_set("x2t", [yd])
                y_dmas.append(yd)

        S("sync", lambda e: e.nop(), deps=y_dmas)
        sched.phase_idx["end"] = len(sched.ops)

        sched.finalize(max_ops)
        block = ctx.enter_context(nc.Block())
        block.tensor(lambda t: sched.emit_engine("tensor", t, sems))
        block.scalar(lambda s: sched.emit_engine("scalar", s, sems))
        block.vector(lambda v: sched.emit_engine("vector", v, sems))
        block.gpsimd(lambda g: sched.emit_engine("gpsimd", g, sems))
        block.sync(lambda e: sched.emit_engine("sync", e, sems))

    nc._dev_phase_idx = dict(sched.phase_idx)
    return nc


_CACHED = {}


def _get_program():
    if "nc" not in _CACHED:
        _CACHED["nc"] = build_program()
    return _CACHED["nc"]


def _pack_inputs(inputs):
    f = np.float32
    Wq = np.asarray(inputs["L0_attn_Wq"], f)
    Wk = np.asarray(inputs["L0_attn_Wk"], f)
    Wv = np.asarray(inputs["L0_attn_Wv"], f)
    Wo = np.asarray(inputs["L0_attn_Wo"], f)
    qn = np.asarray(inputs["L0_attn_qn_w"], f)
    kn = np.asarray(inputs["L0_attn_kn_w"], f)
    rm = np.asarray(inputs["L0_ffn_rail_maker"], f)
    ru = np.asarray(inputs["L0_ffn_rail_up_proj"], f)
    Wc = np.asarray(inputs["L0_ffn_w_cproj"], f)

    common = {
        "wqT": np.ascontiguousarray(Wq.T),
        "wkT": np.ascontiguousarray(Wk.T),
        "wvT": np.ascontiguousarray(Wv.T),
        "woT": np.ascontiguousarray(Wo.T),
        "wcT": np.ascontiguousarray(Wc.T),
    }
    Y = np.zeros((8, 4, 128, 128), f)
    for kc in range(8):
        for hh in range(4):
            for rr in range(8):
                Y[kc, hh, 16 * rr:16 * rr + 16, 16 * rr:16 * rr + 16] = \
                    rm[hh, 8 * kc + rr]
    common["railsY"] = np.ascontiguousarray(
        Y.transpose(2, 0, 1, 3).reshape(128, 4096))
    Z = np.zeros((8, 4, 4, 128, 128), f)
    for kc in range(8):
        for hh in range(4):
            for m in range(4):
                for rp in range(2):
                    r = 8 * kc + 2 * m + rp
                    Z[kc, hh, m, 16 * (2 * m + rp):16 * (2 * m + rp) + 16,
                      64 * rp:64 * rp + 64] = ru[hh, r].T
    Z2 = Z.reshape(4, 2, 4, 4, 128, 128)
    common["railsZ"] = np.ascontiguousarray(
        Z2.transpose(4, 2, 0, 1, 3, 5).reshape(128, 16384))
    za = np.zeros((1, 128), f); za[0, :64] = qn
    zbv = np.zeros((1, 128), f); zbv[0, 64:] = qn
    common["qnwA"], common["qnwB"] = za, zbv
    za = np.zeros((1, 128), f); za[0, :64] = kn
    zbv = np.zeros((1, 128), f); zbv[0, 64:] = kn
    common["knwA"], common["knwB"] = za, zbv
    ia = np.zeros((128, 1), f); ia[:64] = 1.0
    ib = np.zeros((128, 1), f); ib[64:] = 1.0
    common["indA"], common["indB"] = ia, ib
    iar = np.zeros((1, 128), f); iar[0, :64] = 1.0
    ibr = np.zeros((1, 128), f); ibr[0, 64:] = 1.0
    common["indAr"], common["indBr"] = iar, ibr
    common["ones1"] = np.ones((1, 128), f)
    common["eye"] = np.eye(128, dtype=f)
    return common


def kernel(**inputs):
    x = np.asarray(inputs["L0_attn_x"], np.float32)
    common = _pack_inputs(inputs)
    nc = _get_program()
    in_maps = []
    for c in range(NCORES):
        m = dict(common)
        m["x"] = np.ascontiguousarray(x[c])
        in_maps.append(m)
    res = run_bass_kernel_spmd(nc, in_maps, core_ids=list(range(NCORES)))
    return np.stack([np.asarray(r["y"], np.float32) for r in res.results], 0)


# version 10
# speedup vs baseline: 1.1523x; 1.1523x over previous
import numpy as np
from contextlib import ExitStack

import concourse.bass as bass
import concourse.mybir as mybir
from concourse.bass_utils import run_bass_kernel_spmd

F32 = mybir.dt.float32
BF16 = mybir.dt.bfloat16
AF = mybir.ActivationFunctionType
ALU = mybir.AluOpType
AX = mybir.AxisListType

NCORES = 8
N = 4096
D = 1024
DH = 64
HP = 8
NB = 8
NST = 32
SCALE = DH ** -0.5
MARGIN = 25.0
LN_EPS = 1e-5
RMS_EPS = float(np.finfo(np.float32).eps)

DEBUG = False

NSLOT = 4
_DMAQ = ("DMAS", "DMAG", "DMAA")
SEMS = ("PE", "ACT", "DVE", "POOL", "SP") + tuple(
    "%s%d" % (q, s) for q in _DMAQ for s in range(NSLOT))
ENG2CMP = {"tensor": "PE", "scalar": "ACT", "vector": "DVE",
           "gpsimd": "POOL", "sync": "SP"}
ENG2DMA = {"sync": "DMAS", "gpsimd": "DMAG", "scalar": "DMAA"}


class Op:
    __slots__ = ("eng", "fn", "deps", "dma", "tiny", "pre_drain",
                 "needs_inc", "marker", "sem", "idx", "slot_prev")


class Sched:
    def __init__(self):
        self.ops = []
        self.phase_idx = {}
        self._qcount = {}
        self._slot_last = {}

    def add(self, eng, fn, deps=(), dma=False, tiny=False, pre_drain=False):
        op = Op()
        op.eng = eng
        op.fn = fn
        flat = []
        for d in deps:
            if d is None:
                continue
            if isinstance(d, (list, tuple)):
                flat.extend(x for x in d if x is not None)
            else:
                flat.append(d)
        op.deps = flat
        op.dma = dma
        op.tiny = tiny
        op.pre_drain = pre_drain
        op.needs_inc = dma
        op.marker = None
        op.slot_prev = None
        op.idx = len(self.ops)
        for d in op.deps:
            assert d.idx < op.idx, "dep must precede op"
        if dma:
            base = ENG2DMA[eng]
            k = self._qcount.get(base, 0)
            self._qcount[base] = k + 1
            slot = k % NSLOT
            op.sem = "%s%d" % (base, slot)
            op.slot_prev = self._slot_last.get((base, slot))
            self._slot_last[(base, slot)] = op
        else:
            op.sem = ENG2CMP[eng]
        self.ops.append(op)
        return op

    def finalize(self, max_ops=None):
        if max_ops is None:
            max_ops = len(self.ops)
        self.max_ops = max_ops
        for op in self.ops[:max_ops]:
            for d in op.deps:
                if d.dma or d.eng != op.eng:
                    d.needs_inc = True
        counts = {s: 0 for s in SEMS}
        for op in self.ops[:max_ops]:
            if op.needs_inc:
                counts[op.sem] += 16 if op.dma else 1
                op.marker = counts[op.sem]

    def emit_engine(self, eng_name, eng, sems):
        wm = {}
        for op in self.ops[:self.max_ops]:
            if op.eng != eng_name:
                continue
            if op.dma and op.slot_prev is not None:
                p = op.slot_prev
                if p.marker > wm.get(p.sem, 0):
                    eng.wait_ge(sems[p.sem], p.marker)
                    wm[p.sem] = p.marker
            for d in op.deps:
                if d.eng == eng_name and not d.dma:
                    continue
                if d.marker > wm.get(d.sem, 0):
                    eng.wait_ge(sems[d.sem], d.marker)
                    wm[d.sem] = d.marker
            if op.pre_drain:
                eng.drain()
            inst = op.fn(eng)
            if op.needs_inc:
                if op.dma:
                    inst.then_inc(sems[op.sem], 16)
                elif op.tiny:
                    eng.drain().then_inc(sems[op.sem], 1)
                else:
                    inst.then_inc(sems[op.sem], 1)


def build_program(max_ops=None):
    nc = bass.Bass("TRN2", target_bir_lowering=False)

    def dram_in(name, shape):
        return nc.dram_tensor(name, shape, F32, kind="ExternalInput").ap()

    x_d = dram_in("x", [N, D])
    wqT_d = dram_in("wqT", [D, D])
    wkT_d = dram_in("wkT", [D, D])
    wvT_d = dram_in("wvT", [D, D])
    woT_d = dram_in("woT", [D, D])
    wcT_d = dram_in("wcT", [16 * D, D])
    railsY_d = dram_in("railsY", [128, 4096])
    railsZ_d = dram_in("railsZ", [128, 16384])
    qnwA_d = dram_in("qnwA", [1, 128]); qnwB_d = dram_in("qnwB", [1, 128])
    knwA_d = dram_in("knwA", [1, 128]); knwB_d = dram_in("knwB", [1, 128])
    indA_d = dram_in("indA", [128, 1]); indB_d = dram_in("indB", [128, 1])
    indAr_d = dram_in("indAr", [1, 128]); indBr_d = dram_in("indBr", [1, 128])
    ones1_d = dram_in("ones1", [1, 128])
    eye_d = dram_in("eye", [128, 128])

    y_d = nc.dram_tensor("y", [N, D], F32, kind="ExternalOutput").ap()

    ik = "ExternalOutput" if DEBUG else "Internal"
    qT_h = nc.dram_tensor("qT_h", [D, N], BF16, kind=ik).ap()
    kT_h = nc.dram_tensor("kT_h", [D, N], BF16, kind=ik).ap()
    v_h = nc.dram_tensor("v_h", [D, 32 * 130], BF16, kind=ik).ap()
    oT_h = nc.dram_tensor("oT_h", [D, N], BF16, kind=ik).ap()
    x2_h = nc.dram_tensor("x2_h", [N, D], F32, kind=ik).ap()
    h_h = nc.dram_tensor("h_h", [16 * D, N], BF16, kind=ik).ap()
    wcb_h = nc.dram_tensor("wcb_h", [16 * D, D], BF16, kind="Internal").ap()

    sched = Sched()
    S = sched.add
    guards = {}

    def g_get(key):
        return guards.get(key, [])

    def g_set(key, ops_):
        guards[key] = [o for o in ops_ if o is not None]

    with ExitStack() as ctx:
        def sb(name, shape, dt):
            return ctx.enter_context(nc.sbuf_tensor(name, shape, dt))

        BF1 = sb("BF1", [128, 32768], BF16)
        BF2 = sb("BF2", [128, 32768], BF16)
        BF3 = sb("BF3", [128, 16384], BF16)
        FA = sb("FA", [128, 8192], F32)
        SM = sb("SM", [128, 2048], F32)
        eye_s = sb("eye_s", [128, 128], F32)
        ones1_s = sb("ones1_s", [1, 128], F32)
        indA_s = sb("indA_s", [128, 1], F32)
        indB_s = sb("indB_s", [128, 1], F32)
        indAr_s = sb("indAr_s", [1, 128], F32)
        indBr_s = sb("indBr_s", [1, 128], F32)
        qnwA_s = sb("qnwA_s", [1, 128], F32)
        qnwB_s = sb("qnwB_s", [1, 128], F32)
        knwA_s = sb("knwA_s", [1, 128], F32)
        knwB_s = sb("knwB_s", [1, 128], F32)

        pb = [ctx.enter_context(nc.psum_tensor("pb%d" % i, [128, 512], F32))
              for i in range(8)]

        sems = {s: ctx.enter_context(nc.semaphore(s)) for s in SEMS}

        o_epsrms, o_epsln = 0, 1
        o_mean, o_var, o_lnv, o_rstd = 2, 4, 6, 8
        o_sum0, o_sum1, o_ssq0, o_ssq1, o_m2 = 10, 12, 14, 16, 18
        o_rstdA, o_rstdB = 32, 544
        o_sstat = 1056
        o_maxes = 1568
        o_maxhead = 1824
        o_craw, o_biasv = 1856, 1872
        o_biasT = 1888
        o_recipA, o_recipB = 1904, 1920
        o_recipA, o_recipB = o_rstdA, o_rstdB

        FA_x0, FA_x1 = 0, 1024
        FA_xn = (2048, 3072)
        FA_sq = 4096
        FA_sq2 = 5120
        FA_bc = 5632
        FA_att = 6144
        FA_x2 = 7168

        B3_W = 0
        B3_vaug = 2048
        B3_wcs = 10304
        B3_qf = 12352
        B3_ot = 13376
        B3_y0 = 12352
        B3_railsY = 2048
        B3_railsZ = 6400
        B3_wo = 2048
        B3_oTt = 0
        B3_wcb = 0
        B3_hg = 2048

        B1_qT = (0, 4096)
        B1_kT = (8192, 12288)
        B1_vau = (16384, 20544)

        def mm(dst, lhsT, rhs, start=True, stop=True, tile=None, deps=()):
            return S("tensor",
                     lambda t, d=dst, l=lhsT, r=rhs, st=start, sp=stop,
                     tp=tile: t.matmul(d, l, r, start=st, stop=sp,
                                       tile_position=tp),
                     deps=deps)

        def tr(dst, src, deps=()):
            return S("tensor",
                     lambda t, d=dst, s=src: t.transpose(d, s, eye_s[:]),
                     deps=deps)

        def act(out, in_, func, bias=None, scale=None, accum=None, deps=(),
                tiny=False):
            kw = {}
            if bias is not None:
                kw["bias"] = bias
            if scale is not None:
                kw["scale"] = scale
            if accum is not None:
                kw["accum_out"] = accum
            return S("scalar",
                     lambda s, o=out, i=in_, f=func, k=kw:
                     s.activation(o, i, f, **k),
                     deps=deps, tiny=tiny, pre_drain=tiny)

        def acopy(out, in_, deps=(), tiny=False):
            return S("scalar", lambda s, o=out, i=in_: s.copy(o, i),
                     deps=deps, tiny=tiny, pre_drain=tiny)

        def dve(fn, deps=(), tiny=False):
            return S("vector", fn, deps=deps, tiny=tiny, pre_drain=tiny)

        def dma(queue, dst, src, deps=()):
            return S(queue, lambda e, d=dst, s=src: e.dma_start(d, s),
                     deps=deps, dma=True)

        ld_eye = dma("gpsimd", eye_s[:], eye_d)
        ld_ones1 = dma("sync", ones1_s[:], ones1_d)
        ld_indA = dma("sync", indA_s[:], indA_d)
        ld_indB = dma("sync", indB_s[:], indB_d)
        ld_indAr = dma("sync", indAr_s[:], indAr_d)
        ld_indBr = dma("sync", indBr_s[:], indBr_d)
        ld_qnw = [dma("sync", qnwA_s[:], qnwA_d), dma("sync", qnwB_s[:], qnwB_d)]
        ld_knw = [dma("sync", knwA_s[:], knwA_d), dma("sync", knwB_s[:], knwB_d)]
        ms1 = S("vector", lambda v: v.memset(SM[:, o_epsrms:o_epsrms + 1],
                                             RMS_EPS), tiny=True)
        ms2 = S("vector", lambda v: v.memset(SM[:, o_epsln:o_epsln + 1],
                                             LN_EPS), tiny=True)
        vaug = BF3[:, B3_vaug:B3_vaug + 4160]
        vaug_ones = []
        for jc in range(32):
            for extra in (64, 129):
                vaug_ones.append(
                    S("gpsimd", lambda g, c=B3_vaug + jc * 130 + extra:
                      g.memset(BF3[:, c:c + 1], 1.0), tiny=True))

        xt_tile = [FA[:, FA_x0:FA_x0 + 1024], FA[:, FA_x1:FA_x1 + 1024]]

        def ln_stats(src0, src1, slot, deps0, deps1, from_psum=False):
            mean = SM[:, o_mean + slot:o_mean + slot + 1]
            var = SM[:, o_var + slot:o_var + slot + 1]
            lnv = SM[:, o_lnv + slot:o_lnv + slot + 1]
            rstd = SM[:, o_rstd + slot:o_rstd + slot + 1]
            s0 = SM[:, o_sum0 + slot:o_sum0 + slot + 1]
            s1 = SM[:, o_sum1 + slot:o_sum1 + slot + 1]
            q0 = SM[:, o_ssq0 + slot:o_ssq0 + slot + 1]
            q1 = SM[:, o_ssq1 + slot:o_ssq1 + slot + 1]
            m2 = SM[:, o_m2 + slot:o_m2 + slot + 1]
            sq = FA[:, FA_sq:FA_sq + 1024]
            readers = []
            if src1 is None:
                r1 = dve(lambda v, o=s0, i=src0:
                         v.tensor_reduce(o, i, axis=AX.X, op=ALU.add),
                         deps=deps0, tiny=True)
                a1 = act(sq[:, 0:1024], src0, AF.Square, accum=q0,
                         deps=deps0 + ([r1] if from_psum else []), tiny=True)
                c1, c2 = r1, a1
                readers = [r1, a1]
            else:
                r1 = dve(lambda v, o=s0, i=src0:
                         v.tensor_reduce(o, i, axis=AX.X, op=ALU.add),
                         deps=deps0, tiny=True)
                r2 = dve(lambda v, o=s1, i=src1:
                         v.tensor_reduce(o, i, axis=AX.X, op=ALU.add),
                         deps=deps1, tiny=True)
                a1 = act(sq[:, 0:512], src0, AF.Square, accum=q0,
                         deps=deps0 + ([r1] if from_psum else []), tiny=True)
                a2 = act(sq[:, 512:1024], src1, AF.Square, accum=q1,
                         deps=deps1 + ([r2] if from_psum else []), tiny=True)
                c1 = dve(lambda v, o=s0, a=s0, b=s1: v.tensor_add(o, a, b),
                         deps=[r1, r2], tiny=True)
                c2 = dve(lambda v, o=q0, a=q0, b=q1: v.tensor_add(o, a, b),
                         deps=[a1, a2], tiny=True)
                readers = [r1, r2, a1, a2]
            t1 = dve(lambda v, o=mean, i=s0:
                     v.tensor_scalar(o, i, 1.0 / D, None, ALU.mult),
                     deps=[c1], tiny=True)
            t2 = dve(lambda v, o=var, i=q0:
                     v.tensor_scalar(o, i, 1.0 / D, None, ALU.mult),
                     deps=[c2], tiny=True)
            t3 = dve(lambda v, o=m2, a=mean, b=mean: v.tensor_mul(o, a, b),
                     deps=[t1], tiny=True)
            t4 = dve(lambda v, o=var, a=var, b=m2: v.tensor_sub(o, a, b),
                     deps=[t2, t3], tiny=True)
            a3 = act(lnv, var, AF.Ln, bias=SM[:, o_epsln:o_epsln + 1],
                     deps=[t4, ms2], tiny=True)
            a4 = act(rstd, lnv, AF.Exp, scale=-0.5, deps=[a3], tiny=True)
            return mean, rstd, a4, readers

        tr_slot = {}
        p1_evac = []
        for st in range(NST):
            par = st % 2
            ld = dma("sync", xt_tile[par], x_d[st * 128:(st + 1) * 128, :],
                     deps=g_get(("xt", par)))
            mean, rstd, statop, readers = ln_stats(
                xt_tile[par][:, 0:1024], None, par, [ld], None)
            xn = FA[:, FA_xn[par]:FA_xn[par] + 1024]
            nrm = dve(lambda v, o=xn, i=xt_tile[par], m=mean, r=rstd:
                      v.tensor_scalar(o, i, m, r, ALU.subtract, ALU.mult),
                      deps=[statop] + g_get(("xn", par)))
            g_set(("xt", par), readers + [nrm])
            trs = []
            for c in range(8):
                slot = (st * 8 + c) % 2
                bank = pb[slot]
                t = tr(bank[:, 0:128], xn[:, c * 128:c * 128 + 128],
                       deps=[nrm, ld_eye, tr_slot.get(slot)])
                ev = acopy(BF1[:, c * 4096 + st * 128:
                               c * 4096 + st * 128 + 128],
                           bank[:, 0:128], deps=[t])
                tr_slot[slot] = ev
                trs.append(t)
                p1_evac.append(ev)
            g_set(("xn", par), trs)
        p1_done = p1_evac[-1]
        sched.phase_idx["p1"] = len(sched.ops)

        wt_tile = [BF3[:, B3_W:B3_W + 1024], BF3[:, B3_W + 1024:B3_W + 2048]]
        qf_tile = [BF3[:, B3_qf:B3_qf + 512], BF3[:, B3_qf + 512:B3_qf + 1024]]
        projb = (2, 3)
        statb = (4, 5)
        bcb = 6
        wsrc = {"q": wqT_d, "k": wkT_d, "v": wvT_d}
        whb = {"q": qT_h, "k": kT_h}

        def wslice_ap(wd, hp):
            return wd.rearrange("(kc p) c -> p kc c", p=128)[
                :, :, hp * 128:(hp + 1) * 128]

        def w3d(tile):
            return tile.rearrange("p (kc c) -> p kc c", kc=8)

        maxes_written = []
        last_vdma = None
        for hp in range(HP):
            for dsti, dst in enumerate(("q", "k")):
                wpar = (hp * 2 + dsti) % 2
                ldw = dma("gpsimd", w3d(wt_tile[wpar]),
                          wslice_ap(wsrc[dst], hp),
                          deps=g_get(("wt", wpar)))
                wreaders = []
                for ib in range(NB):
                    ppar = ib % 2
                    bank = pb[projb[ppar]]
                    mmop = None
                    for kc in range(8):
                        mmop = mm(bank[:, :],
                                  wt_tile[wpar][:, kc * 128:kc * 128 + 128],
                                  BF1[:, kc * 4096 + ib * 512:
                                      kc * 4096 + ib * 512 + 512],
                                  start=(kc == 0), stop=(kc == 7),
                                  deps=([ldw, p1_done] +
                                        g_get(("projb", ppar))
                                        if kc == 0 else ()))
                    wreaders.append(mmop)
                    sq = FA[:, FA_sq:FA_sq + 512]
                    sqop = act(sq, bank[:, :], AF.Square,
                               deps=[mmop] + g_get("sqr"))
                    stA = mm(pb[statb[0]][0:1, :], indA_s[:], sq,
                             deps=[sqop, ld_indA] + g_get(("stb", 0)))
                    stB = mm(pb[statb[1]][0:1, :], indB_s[:], sq,
                             deps=[sqop, ld_indB] + g_get(("stb", 1)))
                    g_set("sqr", [stA, stB])
                    rstdA = SM[0:1, o_rstdA:o_rstdA + 512]
                    rstdB = SM[0:1, o_rstdB:o_rstdB + 512]
                    lA = act(rstdA, pb[statb[0]][0:1, :], AF.Ln,
                             bias=SM[0:1, o_epsrms:o_epsrms + 1],
                             scale=1.0 / DH, deps=[stA, ms1])
                    eA = act(rstdA, rstdA, AF.Exp, scale=-0.5, deps=[lA],
                             tiny=True)
                    lB = act(rstdB, pb[statb[1]][0:1, :], AF.Ln,
                             bias=SM[0:1, o_epsrms:o_epsrms + 1],
                             scale=1.0 / DH, deps=[stB, ms1])
                    eB = act(rstdB, rstdB, AF.Exp, scale=-0.5, deps=[lB],
                             tiny=True)
                    g_set(("stb", 0), [lA])
                    g_set(("stb", 1), [lB])
                    wA = qnwA_s if dst == "q" else knwA_s
                    wB = qnwB_s if dst == "q" else knwB_s
                    bc1 = mm(pb[bcb][:, :], wA[:], rstdA, start=True,
                             stop=False,
                             deps=[eA] + ld_qnw + ld_knw + g_get("bcb"))
                    bc2 = mm(pb[bcb][:, :], wB[:], rstdB, start=False,
                             stop=True, deps=[eB])
                    bce = act(FA[:, FA_bc:FA_bc + 512], pb[bcb][:, :],
                              AF.Identity, deps=[bc2] + g_get("fabc"))
                    g_set("bcb", [bce])
                    qfin = qf_tile[ppar]
                    qop = dve(lambda v, o=qfin, a=bank[:, :],
                              b=FA[:, FA_bc:FA_bc + 512]:
                              v.tensor_tensor(o, a, b, op=ALU.mult),
                              deps=[bce] + g_get(("qf", ppar)))
                    guards["fabc"] = [qop]
                    g_set(("projb", ppar), [qop, sqop])
                    st_dma = dma("sync",
                                 whb[dst][hp * 128:(hp + 1) * 128,
                                          ib * 512:(ib + 1) * 512],
                                 qfin, deps=[qop])
                    sq2 = FA[:, FA_sq2:FA_sq2 + 512]
                    sqop2 = act(sq2, qfin, AF.Square,
                                deps=[qop] + g_get("sq2r"))
                    g_set(("qf", ppar), [st_dma, sqop2])
                    fsA = mm(pb[statb[0]][0:1, :], indA_s[:], sq2,
                             deps=[sqop2] + g_get(("stb", 0)))
                    fsB = mm(pb[statb[1]][0:1, :], indB_s[:], sq2,
                             deps=[sqop2] + g_get(("stb", 1)))
                    g_set("sq2r", [fsA, fsB])
                    for hd, fs, bk in ((0, fsA, statb[0]), (1, fsB, statb[1])):
                        slot = (dsti * 16 + hp * 2 + hd) * 8 + ib
                        mx = dve(lambda v,
                                 o=SM[0:1, o_maxes + slot:o_maxes + slot + 1],
                                 i=pb[bk][0:1, :]:
                                 v.tensor_reduce(o, i, axis=AX.X, op=ALU.max),
                                 deps=[fs], tiny=True)
                        maxes_written.append(mx)
                        g_set(("stb", hd), [mx])
                g_set(("wt", wpar), wreaders)

            wpar = hp % 2
            ldw = dma("gpsimd", w3d(wt_tile[wpar]), wslice_ap(wvT_d, hp),
                      deps=g_get(("wt", wpar)))
            wreaders = []
            vops = []
            for ib in range(NB):
                ppar = ib % 2
                bank = pb[projb[ppar]]
                mmop = None
                for kc in range(8):
                    mmop = mm(bank[:, :],
                              wt_tile[wpar][:, kc * 128:kc * 128 + 128],
                              BF1[:, kc * 4096 + ib * 512:
                                  kc * 4096 + ib * 512 + 512],
                              start=(kc == 0), stop=(kc == 7),
                              deps=([ldw, p1_done] + g_get(("projb", ppar))
                                    if kc == 0 else ()))
                wreaders.append(mmop)
                vt = FA[:, FA_att:FA_att + 512]
                ev = acopy(vt, bank[:, :], deps=[mmop] + g_get("vt"))
                g_set(("projb", ppar), [ev])
                trs = []
                for c in range(4):
                    jc = ib * 4 + c
                    slot = jc % 2
                    bank2 = pb[slot]
                    t = tr(bank2[:, 0:128],
                           vt[:, c * 128:c * 128 + 128],
                           deps=[ev, ld_eye, tr_slot.get(slot)] +
                                g_get("vaug_dma"))
                    e1 = acopy(vaug[:, jc * 130:jc * 130 + 64],
                               bank2[:, 0:64], deps=[t])
                    e2 = acopy(vaug[:, jc * 130 + 65:jc * 130 + 129],
                               bank2[:, 64:128], deps=[t])
                    tr_slot[slot] = e2
                    trs.append(t)
                    vops.extend([e1, e2])
                g_set("vt", trs)
            vd = dma("sync", v_h[hp * 128:(hp + 1) * 128, :], vaug,
                     deps=vops + vaug_ones)
            g_set("vaug_dma", [vd])
            last_vdma = vd
            g_set(("wt", wpar), wreaders + [vd])

        mh_ops = []
        for i in range(32):
            op = dve(lambda v, o=SM[0:1, o_maxhead + i:o_maxhead + i + 1],
                     s=SM[0:1, o_maxes + i * 8:o_maxes + i * 8 + 8]:
                     v.tensor_reduce(o, s, axis=AX.X, op=ALU.max),
                     deps=[], tiny=True)
            mh_ops.append(op)
        craw = SM[0:1, o_craw:o_craw + 16]
        cop = dve(lambda v, o=craw, a=SM[0:1, o_maxhead:o_maxhead + 16],
                  b=SM[0:1, o_maxhead + 16:o_maxhead + 32]:
                  v.tensor_mul(o, a, b), deps=[], tiny=True)
        csq = act(craw, craw, AF.Sqrt, deps=[cop], tiny=True)
        biasv = SM[0:1, o_biasv:o_biasv + 16]
        bop = dve(lambda v, o=biasv, i=craw:
                  v.tensor_scalar(o, i, -SCALE, MARGIN, ALU.mult, ALU.add),
                  deps=[csq], tiny=True)
        bcm = mm(pb[bcb][:, 0:16], ones1_s[:], biasv,
                 deps=[bop, ld_ones1] + g_get("bcb"))
        biasT = SM[:, o_biasT:o_biasT + 16]
        bias_ready = acopy(biasT, pb[bcb][:, 0:16], deps=[bcm], tiny=True)
        sched.phase_idx["p15"] = len(sched.ops)


        qT_buf = [BF1[:, B1_qT[0]:B1_qT[0] + 4096],
                  BF1[:, B1_qT[1]:B1_qT[1] + 4096]]
        kT_buf = [BF1[:, B1_kT[0]:B1_kT[0] + 4096],
                  BF1[:, B1_kT[1]:B1_kT[1] + 4096]]
        va_buf = [BF1[:, B1_vau[0]:B1_vau[0] + 4160],
                  BF1[:, B1_vau[1]:B1_vau[1] + 4160]]
        Ablk = [BF2[:, 0:16384], BF2[:, 16384:32768]]
        ot_tile = [BF3[:, B3_ot:B3_ot + 512], BF3[:, B3_ot + 512:B3_ot + 1024]]
        SCB = (0, 1, 2)
        AVB = {0: (3, 4), 1: (5, 6)}
        RCB = 7
        for s in range(3):
            g_set(("scb3", s), [bias_ready])
        for hd in range(2):
            for p in range(2):
                g_set(("avb", hd, p), [bias_ready])
        g_set("rcb", [bias_ready])
        g_set(("qkv", 0), [last_vdma])
        g_set(("qkv", 1), [last_vdma])
        Ause = {}
        qkv_loaded = [None, None]

        def load_hp(hp):
            par = hp % 2
            dep = g_get(("qkv", par))
            l1 = dma("sync", qT_buf[par], qT_h[hp * 128:(hp + 1) * 128, :],
                     deps=dep)
            l2 = dma("sync", kT_buf[par], kT_h[hp * 128:(hp + 1) * 128, :],
                     deps=dep)
            l3 = dma("sync", va_buf[par], v_h[hp * 128:(hp + 1) * 128, :],
                     deps=dep)
            qkv_loaded[par] = [l1, l2, l3]

        g_set("p2bc", g_get("fabc"))
        last_od = [None, None]
        step = 0
        load_hp(0)
        for hp in range(HP):
            par = hp % 2
            if hp + 1 < HP:
                load_hp(hp + 1)
            loads = qkv_loaded[par]
            av_done = {}
            pend = []
            eop_of = {}

            def emit_tail(ib):
                apar = ib % 2
                otp = ib % 2
                rsb = [SM[0:1, o_recipA:o_recipA + 512],
                       SM[0:1, o_recipB:o_recipB + 512]]
                cc = []
                for hd in range(2):
                    cc.append(acopy(rsb[hd], pb[AVB[hd][apar]][64:65, :],
                                    deps=[av_done[(ib, hd)]] +
                                         g_get(("rrow", hd))))
                bc1 = mm(pb[RCB][:, :], indAr_s[:], rsb[0], start=True,
                         stop=False,
                         deps=[cc[0], ld_indAr, ld_indBr] + g_get("rcb"))
                bc2 = mm(pb[RCB][:, :], indBr_s[:], rsb[1], start=False,
                         stop=True, deps=[cc[1]])
                g_set(("rrow", 0), [bc1])
                g_set(("rrow", 1), [bc2])
                dbc = FA[:, FA_bc:FA_bc + 512]
                bce = act(dbc, pb[RCB][:, :], AF.Identity,
                          deps=[bc2] + g_get("p2bc"))
                g_set("rcb", [bce])
                rcop = dve(lambda v, o=dbc, i=dbc: v.reciprocal(o, i),
                           deps=[bce])
                oops = []
                for hd in range(2):
                    oop = dve(lambda v,
                              o=ot_tile[otp][hd * 64:hd * 64 + 64, :],
                              a=pb[AVB[hd][apar]][0:64, :],
                              b=dbc[hd * 64:hd * 64 + 64, :]:
                              v.tensor_tensor(o, a, b, op=ALU.mult),
                              deps=[rcop] + g_get(("ot", otp)))
                    g_set(("avb", hd, apar), [oop, cc[hd]])
                    oops.append(oop)
                g_set("p2bc", oops + [rcop])
                od = dma("sync", oT_h[hp * 128:(hp + 1) * 128,
                                      ib * 512:(ib + 1) * 512],
                         ot_tile[otp], deps=oops)
                g_set(("ot", otp), [od])
                last_od[otp] = od

            def issue_avop(ib, jc, hd):
                apar = ib % 2
                vcol = jc * 130 + hd * 65
                avop = mm(pb[AVB[hd][apar]][0:65, :],
                          va_buf[par][:, vcol:vcol + 65],
                          Ablk[hd][:, jc * 512:jc * 512 + 512],
                          start=(jc == 0), stop=(jc == 31),
                          deps=[eop_of[(ib, jc, hd)], loads[2]] +
                               (g_get(("avb", hd, apar)) if jc == 0 else []))
                Ause[(hd, jc)] = avop
                if jc == 31:
                    av_done[(ib, hd)] = avop
                    if hd == 1:
                        emit_tail(ib)
                return avop

            for ib in range(NB):
                for jc in range(32):
                    for hd in range(2):
                        s = step % 3
                        step += 1
                        sb_ = pb[SCB[s]]
                        prow = hd * 64
                        scop = mm(sb_[:, :],
                                  kT_buf[par][prow:prow + 64,
                                              jc * 128:jc * 128 + 128],
                                  qT_buf[par][prow:prow + 64,
                                              ib * 512:ib * 512 + 512],
                                  tile=(prow, 0),
                                  deps=[loads[0], loads[1]] +
                                       g_get(("scb3", s)))
                        adst = Ablk[hd][:, jc * 512:jc * 512 + 512]
                        eop = act(adst, sb_[:, :], AF.Exp,
                                  bias=biasT[:, hp * 2 + hd:hp * 2 + hd + 1],
                                  scale=SCALE,
                                  deps=[scop, bias_ready,
                                        Ause.get((hd, jc))])
                        g_set(("scb3", s), [eop])
                        eop_of[(ib, jc, hd)] = eop
                        pend.append((ib, jc, hd))
                        if len(pend) >= 6:
                            for _ in range(3):
                                issue_avop(*pend.pop(0))
            while pend:
                issue_avop(*pend.pop(0))
            g_set(("qkv", par), [av_done[(NB - 1, 0)], av_done[(NB - 1, 1)]])
        p2_last_od = [d for d in last_od if d is not None]
        g_set(("scb", 1, 0), p2_last_od)
        g_set(("scb", 1, 1), p2_last_od)
        sched.phase_idx["p2"] = len(sched.ops)

        oTt = [BF3[:, B3_oTt:B3_oTt + 1024],
               BF3[:, B3_oTt + 1024:B3_oTt + 2048]]
        wo_sb = BF3[:, B3_wo:B3_wo + 8192]
        ldwo = []
        for c in range(8):
            deps = []
            if c == 0:
                deps = (g_get(("qkv", 0)) + g_get(("qkv", 1)) +
                        p2_last_od + g_get(("ot", 0)) + g_get(("ot", 1)) +
                        g_get(("wcs", 0)) + g_get(("wcs", 1)) +
                        g_get(("wt", 0)) + g_get(("wt", 1)) +
                        g_get("vaug_dma"))
            ldwo.append(dma("gpsimd", wo_sb[:, c * 1024:(c + 1) * 1024],
                            woT_d[c * 128:(c + 1) * 128, :], deps=deps))
        wob = (0, 1)
        g_set(("wob", 0), [bias_ready] + p2_last_od)
        g_set(("wob", 1), [bias_ready] + p2_last_od)
        oT_src = oT_h.rearrange("(kc p) t -> p kc t", p=128)
        ln2_evac = []
        x2st_done = []
        tr3_slot = {}
        pend_trs = None
        for st in range(NST):
            par = st % 2
            lo = dma("sync", w3d(oTt[par]),
                     oT_src[:, :, st * 128:(st + 1) * 128],
                     deps=g_get(("oTt", par)) + p2_last_od +
                          (g_get(("wt", 0)) + g_get(("wt", 1))
                           if st < 2 else []))
            lx = dma("sync", xt_tile[par], x_d[st * 128:(st + 1) * 128, :],
                     deps=g_get(("xt", par)))
            mops = []
            for nh in range(2):
                mmop = None
                for kc in range(8):
                    mmop = mm(pb[wob[nh]][:, :],
                              oTt[par][:, kc * 128:kc * 128 + 128],
                              wo_sb[:, kc * 1024 + nh * 512:
                                    kc * 1024 + nh * 512 + 512],
                              start=(kc == 0), stop=(kc == 7),
                              deps=([ldwo[kc], lo] + g_get(("wob", nh))
                                    if kc == 0 else [ldwo[kc]]))
                mops.append(mmop)
            g_set(("oTt", par), mops)
            attb = (FA[:, FA_att:FA_att + 1024] if par == 0
                    else FA[:, FA_sq2:FA_sq2 + 1024])
            e0 = acopy(attb[:, 0:512], pb[wob[0]][:, :],
                       deps=[mops[0]] + g_get(("p3evt", par)))
            e1 = acopy(attb[:, 512:1024], pb[wob[1]][:, :],
                       deps=[mops[1]] + g_get(("p3evt", par)))
            g_set(("wob", 0), [e0])
            g_set(("wob", 1), [e1])
            mean, rstd, statop, readers = ln_stats(
                attb[:, 0:1024], None, par, [e0, e1], None)
            nrm = dve(lambda v, o=attb, i=attb, m=mean, r=rstd:
                      v.tensor_scalar(o, i, m, r, ALU.subtract, ALU.mult),
                      deps=[statop])
            x2t = FA[:, FA_x2:FA_x2 + 1024]
            xa = dve(lambda v, o=x2t, a=xt_tile[par], b=attb:
                     v.tensor_add(o, a, b),
                     deps=[lx, nrm] + g_get("x2t"))
            g_set(("p3evt", par), readers + [nrm, xa])
            g_set(("xt", par), [xa])
            sx2 = dma("scalar", x2_h[st * 128:(st + 1) * 128, :], x2t,
                      deps=[xa])
            x2st_done.append(sx2)
            mean2, rstd2, statop2, readers2 = ln_stats(
                x2t[:, 0:1024], None, par, [xa], None)
            ln2f = FA[:, FA_xn[par]:FA_xn[par] + 1024]
            l2 = dve(lambda v, o=ln2f, i=x2t, m=mean2, r=rstd2:
                     v.tensor_scalar(o, i, m, r, ALU.subtract, ALU.mult),
                     deps=[statop2] + g_get(("xn", par)))
            g_set("x2t", [sx2] + readers2 + [l2])

            def emit_trs(st_, l2_, ln2f_):
                par_ = st_ % 2
                trs = []
                for c in range(8):
                    slot = (st_ * 8 + c) % 2
                    bank = pb[slot + 2]
                    prior = tr3_slot.get(slot)
                    extra = ([] if prior is not None else
                             g_get(("scb", 1, 0)) + g_get(("scb", 1, 1)))
                    t = tr(bank[:, 0:128], ln2f_[:, c * 128:c * 128 + 128],
                           deps=[l2_, prior] + extra)
                    ev = acopy(BF1[:, c * 4096 + st_ * 128:
                                   c * 4096 + st_ * 128 + 128],
                               bank[:, 0:128], deps=[t])
                    tr3_slot[slot] = ev
                    trs.append(t)
                    ln2_evac.append(ev)
                g_set(("xn", par_), trs)

            if pend_trs is not None:
                emit_trs(*pend_trs)
            pend_trs = (st, l2, ln2f)
        emit_trs(*pend_trs)
        p3_done = ln2_evac[-1]
        sched.phase_idx["p3"] = len(sched.ops)

        railsY_sb = BF3[:, B3_railsY:B3_railsY + 4096]
        ldry = dma("gpsimd", railsY_sb, railsY_d, deps=[p3_done, ldwo[-1]])
        rz_tile = [BF3[:, B3_railsZ:B3_railsZ + 1024],
                   BF3[:, B3_railsZ + 1024:B3_railsZ + 2048]]
        y_tile = [BF3[:, B3_y0:B3_y0 + 512], BF3[:, B3_y0 + 512:B3_y0 + 1024]]
        yb = (0, 1)
        zb = (2, 3)
        g_set(("yb", 0), [p3_done]); g_set(("yb", 1), [p3_done])
        g_set(("zb", 0), [p3_done]); g_set(("zb", 1), [p3_done])
        g_set(("ytl", 0), [p3_done]); g_set(("ytl", 1), [p3_done])
        g_set(("qfdead", 0), g_get(("qf", 0)))
        hstore = {}
        slab_last = None
        for rb in range(16):
            hh, qr = rb // 4, rb % 4
            rpar = rb % 2
            lz = dma("gpsimd", rz_tile[rpar],
                     railsZ_d[:, rb * 1024:(rb + 1) * 1024],
                     deps=g_get(("rz", rpar)) + [p3_done, ldwo[-1]])
            gel_last = [None] * 8
            zreads = {0: [], 1: []}
            for tb4 in range(NB):
                yev = []
                for kr in range(2):
                    kcx = 2 * qr + kr
                    ym = mm(pb[yb[kr]][:, :],
                            railsY_sb[:, kcx * 512 + hh * 128:
                                      kcx * 512 + hh * 128 + 128],
                            BF1[:, kcx * 4096 + tb4 * 512:
                                kcx * 4096 + tb4 * 512 + 512],
                            deps=[ldry, p3_done] + g_get(("yb", kr)))
                    ye = acopy(y_tile[kr], pb[yb[kr]][:, :],
                               deps=[ym] + g_get(("ytl", kr)) +
                                    (g_get(("qf", 0)) + g_get(("qf", 1))
                                     if rb == 0 and tb4 == 0 else []))
                    g_set(("yb", kr), [ye])
                    yev.append(ye)
                zr = {0: [], 1: []}
                for fc in range(8):
                    kr, m = fc // 4, fc % 4
                    zpar = fc % 2
                    zm = mm(pb[zb[zpar]][:, :],
                            rz_tile[rpar][:, fc * 128:fc * 128 + 128],
                            y_tile[kr],
                            deps=[lz, yev[kr]] + g_get(("zb", zpar)))
                    zr[kr].append(zm)
                    gl = act(BF2[:, fc * 4096 + tb4 * 512:
                                 fc * 4096 + tb4 * 512 + 512],
                             pb[zb[zpar]][:, :], AF.Gelu,
                             deps=[zm] + g_get(("slab", fc)) +
                                  ([Ause.get((fc // 4, 31))]
                                   if rb == 0 and tb4 == 0 else []))
                    g_set(("zb", zpar), [gl])
                    gel_last[fc] = gl
                for kr in range(2):
                    g_set(("ytl", kr), zr[kr])
                    zreads[kr].extend(zr[kr])
            sds = []
            for fc in range(8):
                fkc = hh * 32 + qr * 8 + fc
                sd = dma("sync", h_h[fkc * 128:(fkc + 1) * 128, :],
                         BF2[:, fc * 4096:(fc + 1) * 4096],
                         deps=[gel_last[fc]])
                g_set(("slab", fc), [sd])
                hstore[fkc] = sd
                sds.append(sd)
            g_set(("rz", rpar), zreads[0] + zreads[1])
            slab_last = sds[-1]

        sched.phase_idx["p4a"] = len(sched.ops)
        wcb_tile = [BF3[:, B3_wcb:B3_wcb + 1024],
                    BF3[:, B3_wcb + 1024:B3_wcb + 2048],
                    BF3[:, 8448:8448 + 1024],
                    BF3[:, 9472:9472 + 1024]]
        hg_tile = [BF3[:, B3_hg:B3_hg + 2048],
                   BF3[:, B3_hg + 2048:B3_hg + 4096]]
        h_src = h_h.rearrange("(kg g p) t -> kg p g t", p=128, g=4)
        for b in range(8):
            g_set(("accb", b), [slab_last])
        g_set(("hg", 0), [slab_last]); g_set(("hg", 1), [slab_last])
        for w in range(4):
            g_set(("wcbt", w), [slab_last])
        y_dmas = []
        for stg in range(8):
            acc_stops = []
            for kg in range(32):
                gpar = kg % 2
                lh = dma("sync",
                         hg_tile[gpar].rearrange("p (g t) -> p g t", g=4),
                         h_src[kg, :, :, stg * 512:(stg + 1) * 512],
                         deps=g_get(("hg", gpar)) +
                              [hstore[kg * 4 + g] for g in range(4)])
                hg_readers = []
                for kcl in range(4):
                    kc = kg * 4 + kcl
                    wpar = kc % 4
                    lw = dma("gpsimd", wcb_tile[wpar],
                             wcT_d[kc * 128:(kc + 1) * 128, :],
                             deps=g_get(("wcbt", wpar)))
                    wreaders = []
                    for st4 in range(4):
                        for nh in range(2):
                            bnk = st4 * 2 + nh
                            op = mm(pb[bnk][:, :],
                                    hg_tile[gpar][:,
                                                  kcl * 512 + st4 * 128:
                                                  kcl * 512 + st4 * 128 + 128],
                                    wcb_tile[wpar][:, nh * 512:nh * 512 + 512],
                                    start=(kc == 0), stop=(kc == 127),
                                    deps=[lh, lw] +
                                         (g_get(("accb", bnk))
                                          if kc == 0 else []))
                            wreaders.append(op)
                            hg_readers.append(op)
                            if kc == 127:
                                acc_stops.append(op)
                    g_set(("wcbt", wpar), wreaders)
                g_set(("hg", gpar), hg_readers)
            for st4 in range(4):
                st = stg * 4 + st4
                par = st % 2
                m0, m1 = acc_stops[st4 * 2], acc_stops[st4 * 2 + 1]
                evt = FA[:, FA_xn[par]:FA_xn[par] + 1024]
                e0 = acopy(evt[:, 0:512], pb[st4 * 2][:, :],
                           deps=[m0] + g_get(("evt", par)))
                e1 = acopy(evt[:, 512:1024], pb[st4 * 2 + 1][:, :],
                           deps=[m1] + g_get(("evt", par)))
                g_set(("accb", st4 * 2), [e0])
                g_set(("accb", st4 * 2 + 1), [e1])
                lx2 = dma("scalar", xt_tile[par],
                          x2_h[st * 128:(st + 1) * 128, :],
                          deps=g_get(("xt", par)) + [x2st_done[st]])
                mean, rstd, statop, readers = ln_stats(
                    evt[:, 0:1024], None, par, [e0, e1], None)
                att = FA[:, FA_att:FA_att + 1024]
                n0 = dve(lambda v, o=att, i=evt, m=mean, r=rstd:
                         v.tensor_scalar(o, i, m, r, ALU.subtract, ALU.mult),
                         deps=[statop] + g_get("p4att"))
                g_set(("evt", par), readers + [n0])
                yt = FA[:, FA_x2:FA_x2 + 1024]
                ya = dve(lambda v, o=yt, a=xt_tile[par], b=att:
                         v.tensor_add(o, a, b),
                         deps=[lx2, n0] + g_get("x2t"))
                g_set("p4att", [ya])
                g_set(("xt", par), [ya])
                yd = dma("scalar", y_d[st * 128:(st + 1) * 128, :], yt,
                         deps=[ya])
                g_set("x2t", [yd])
                y_dmas.append(yd)

        S("sync", lambda e: e.nop(), deps=y_dmas)
        sched.phase_idx["end"] = len(sched.ops)

        sched.finalize(max_ops)
        block = ctx.enter_context(nc.Block())
        block.tensor(lambda t: sched.emit_engine("tensor", t, sems))
        block.scalar(lambda s: sched.emit_engine("scalar", s, sems))
        block.vector(lambda v: sched.emit_engine("vector", v, sems))
        block.gpsimd(lambda g: sched.emit_engine("gpsimd", g, sems))
        block.sync(lambda e: sched.emit_engine("sync", e, sems))

    nc._dev_phase_idx = dict(sched.phase_idx)
    return nc


_CACHED = {}


def _get_program():
    if "nc" not in _CACHED:
        _CACHED["nc"] = build_program()
    return _CACHED["nc"]


def _pack_inputs(inputs):
    f = np.float32
    Wq = np.asarray(inputs["L0_attn_Wq"], f)
    Wk = np.asarray(inputs["L0_attn_Wk"], f)
    Wv = np.asarray(inputs["L0_attn_Wv"], f)
    Wo = np.asarray(inputs["L0_attn_Wo"], f)
    qn = np.asarray(inputs["L0_attn_qn_w"], f)
    kn = np.asarray(inputs["L0_attn_kn_w"], f)
    rm = np.asarray(inputs["L0_ffn_rail_maker"], f)
    ru = np.asarray(inputs["L0_ffn_rail_up_proj"], f)
    Wc = np.asarray(inputs["L0_ffn_w_cproj"], f)

    common = {
        "wqT": np.ascontiguousarray(Wq.T),
        "wkT": np.ascontiguousarray(Wk.T),
        "wvT": np.ascontiguousarray(Wv.T),
        "woT": np.ascontiguousarray(Wo.T),
        "wcT": np.ascontiguousarray(Wc.T),
    }
    Y = np.zeros((8, 4, 128, 128), f)
    for kc in range(8):
        for hh in range(4):
            for rr in range(8):
                Y[kc, hh, 16 * rr:16 * rr + 16, 16 * rr:16 * rr + 16] = \
                    rm[hh, 8 * kc + rr]
    common["railsY"] = np.ascontiguousarray(
        Y.transpose(2, 0, 1, 3).reshape(128, 4096))
    Z = np.zeros((8, 4, 4, 128, 128), f)
    for kc in range(8):
        for hh in range(4):
            for m in range(4):
                for rp in range(2):
                    r = 8 * kc + 2 * m + rp
                    Z[kc, hh, m, 16 * (2 * m + rp):16 * (2 * m + rp) + 16,
                      64 * rp:64 * rp + 64] = ru[hh, r].T
    Z2 = Z.reshape(4, 2, 4, 4, 128, 128)
    common["railsZ"] = np.ascontiguousarray(
        Z2.transpose(4, 2, 0, 1, 3, 5).reshape(128, 16384))
    za = np.zeros((1, 128), f); za[0, :64] = qn
    zbv = np.zeros((1, 128), f); zbv[0, 64:] = qn
    common["qnwA"], common["qnwB"] = za, zbv
    za = np.zeros((1, 128), f); za[0, :64] = kn
    zbv = np.zeros((1, 128), f); zbv[0, 64:] = kn
    common["knwA"], common["knwB"] = za, zbv
    ia = np.zeros((128, 1), f); ia[:64] = 1.0
    ib = np.zeros((128, 1), f); ib[64:] = 1.0
    common["indA"], common["indB"] = ia, ib
    iar = np.zeros((1, 128), f); iar[0, :64] = 1.0
    ibr = np.zeros((1, 128), f); ibr[0, 64:] = 1.0
    common["indAr"], common["indBr"] = iar, ibr
    common["ones1"] = np.ones((1, 128), f)
    common["eye"] = np.eye(128, dtype=f)
    return common


def kernel(**inputs):
    x = np.asarray(inputs["L0_attn_x"], np.float32)
    common = _pack_inputs(inputs)
    nc = _get_program()
    in_maps = []
    for c in range(NCORES):
        m = dict(common)
        m["x"] = np.ascontiguousarray(x[c])
        in_maps.append(m)
    res = run_bass_kernel_spmd(nc, in_maps, core_ids=list(range(NCORES)))
    return np.stack([np.asarray(r["y"], np.float32) for r in res.results], 0)


# version 11
# speedup vs baseline: 1.1816x; 1.0254x over previous
import numpy as np
from contextlib import ExitStack

import concourse.bass as bass
import concourse.mybir as mybir
from concourse.bass_utils import run_bass_kernel_spmd

F32 = mybir.dt.float32
BF16 = mybir.dt.bfloat16
AF = mybir.ActivationFunctionType
ALU = mybir.AluOpType
AX = mybir.AxisListType

NCORES = 8
N = 4096
D = 1024
DH = 64
HP = 8
NB = 8
NST = 32
SCALE = DH ** -0.5
MARGIN = 25.0
LN_EPS = 1e-5
RMS_EPS = float(np.finfo(np.float32).eps)

DEBUG = False

NSLOT = 4
_DMAQ = ("DMAS", "DMAG", "DMAA")
SEMS = ("PE", "ACT", "DVE", "POOL", "SP") + tuple(
    "%s%d" % (q, s) for q in _DMAQ for s in range(NSLOT))
ENG2CMP = {"tensor": "PE", "scalar": "ACT", "vector": "DVE",
           "gpsimd": "POOL", "sync": "SP"}
ENG2DMA = {"sync": "DMAS", "gpsimd": "DMAG", "scalar": "DMAA"}


class Op:
    __slots__ = ("eng", "fn", "deps", "dma", "tiny", "pre_drain",
                 "needs_inc", "marker", "sem", "idx", "slot_prev")


class Sched:
    def __init__(self):
        self.ops = []
        self.phase_idx = {}
        self._qcount = {}
        self._slot_last = {}

    def add(self, eng, fn, deps=(), dma=False, tiny=False, pre_drain=False):
        op = Op()
        op.eng = eng
        op.fn = fn
        flat = []
        for d in deps:
            if d is None:
                continue
            if isinstance(d, (list, tuple)):
                flat.extend(x for x in d if x is not None)
            else:
                flat.append(d)
        op.deps = flat
        op.dma = dma
        op.tiny = tiny
        op.pre_drain = pre_drain
        op.needs_inc = dma
        op.marker = None
        op.slot_prev = None
        op.idx = len(self.ops)
        for d in op.deps:
            assert d.idx < op.idx, "dep must precede op"
        if dma:
            base = ENG2DMA[eng]
            k = self._qcount.get(base, 0)
            self._qcount[base] = k + 1
            slot = k % NSLOT
            op.sem = "%s%d" % (base, slot)
            op.slot_prev = self._slot_last.get((base, slot))
            self._slot_last[(base, slot)] = op
        else:
            op.sem = ENG2CMP[eng]
        self.ops.append(op)
        return op

    def finalize(self, max_ops=None):
        if max_ops is None:
            max_ops = len(self.ops)
        self.max_ops = max_ops
        for op in self.ops[:max_ops]:
            for d in op.deps:
                if d.dma or d.eng != op.eng:
                    d.needs_inc = True
        counts = {s: 0 for s in SEMS}
        for op in self.ops[:max_ops]:
            if op.needs_inc:
                counts[op.sem] += 16 if op.dma else 1
                op.marker = counts[op.sem]

    def emit_engine(self, eng_name, eng, sems):
        wm = {}
        for op in self.ops[:self.max_ops]:
            if op.eng != eng_name:
                continue
            if op.dma and op.slot_prev is not None:
                p = op.slot_prev
                if p.marker > wm.get(p.sem, 0):
                    eng.wait_ge(sems[p.sem], p.marker)
                    wm[p.sem] = p.marker
            for d in op.deps:
                if d.eng == eng_name and not d.dma:
                    continue
                if d.marker > wm.get(d.sem, 0):
                    eng.wait_ge(sems[d.sem], d.marker)
                    wm[d.sem] = d.marker
            if op.pre_drain:
                eng.drain()
            inst = op.fn(eng)
            if op.needs_inc:
                if op.dma:
                    inst.then_inc(sems[op.sem], 16)
                elif op.tiny:
                    eng.drain().then_inc(sems[op.sem], 1)
                else:
                    inst.then_inc(sems[op.sem], 1)


def build_program(max_ops=None):
    nc = bass.Bass("TRN2", target_bir_lowering=False)

    def dram_in(name, shape):
        return nc.dram_tensor(name, shape, F32, kind="ExternalInput").ap()

    x_d = dram_in("x", [N, D])
    wqT_d = dram_in("wqT", [D, D])
    wkT_d = dram_in("wkT", [D, D])
    wvT_d = dram_in("wvT", [D, D])
    woT_d = dram_in("woT", [D, D])
    wcT_d = dram_in("wcT", [16 * D, D])
    railsY_d = dram_in("railsY", [128, 4096])
    railsZ_d = dram_in("railsZ", [128, 16384])
    qnwA_d = dram_in("qnwA", [1, 128]); qnwB_d = dram_in("qnwB", [1, 128])
    knwA_d = dram_in("knwA", [1, 128]); knwB_d = dram_in("knwB", [1, 128])
    indA_d = dram_in("indA", [128, 1]); indB_d = dram_in("indB", [128, 1])
    indAr_d = dram_in("indAr", [1, 128]); indBr_d = dram_in("indBr", [1, 128])
    ones1_d = dram_in("ones1", [1, 128])
    eye_d = dram_in("eye", [128, 128])

    y_d = nc.dram_tensor("y", [N, D], F32, kind="ExternalOutput").ap()

    ik = "ExternalOutput" if DEBUG else "Internal"
    qT_h = nc.dram_tensor("qT_h", [D, N], BF16, kind=ik).ap()
    kT_h = nc.dram_tensor("kT_h", [D, N], BF16, kind=ik).ap()
    v_h = nc.dram_tensor("v_h", [D, 32 * 130], BF16, kind=ik).ap()
    oT_h = nc.dram_tensor("oT_h", [D, N], BF16, kind=ik).ap()
    x2_h = nc.dram_tensor("x2_h", [N, D], F32, kind=ik).ap()
    h_h = nc.dram_tensor("h_h", [16 * D, N], BF16, kind=ik).ap()
    wcb_h = nc.dram_tensor("wcb_h", [16 * D, D], BF16, kind="Internal").ap()

    sched = Sched()
    S = sched.add
    guards = {}

    def g_get(key):
        return guards.get(key, [])

    def g_set(key, ops_):
        guards[key] = [o for o in ops_ if o is not None]

    with ExitStack() as ctx:
        def sb(name, shape, dt):
            return ctx.enter_context(nc.sbuf_tensor(name, shape, dt))

        BF1 = sb("BF1", [128, 32768], BF16)
        BF2 = sb("BF2", [128, 32768], BF16)
        BF3 = sb("BF3", [128, 16384], BF16)
        FA = sb("FA", [128, 8192], F32)
        SM = sb("SM", [128, 2048], F32)
        eye_s = sb("eye_s", [128, 128], F32)
        ones1_s = sb("ones1_s", [1, 128], F32)
        indA_s = sb("indA_s", [128, 1], F32)
        indB_s = sb("indB_s", [128, 1], F32)
        indAr_s = sb("indAr_s", [1, 128], F32)
        indBr_s = sb("indBr_s", [1, 128], F32)
        qnwA_s = sb("qnwA_s", [1, 128], F32)
        qnwB_s = sb("qnwB_s", [1, 128], F32)
        knwA_s = sb("knwA_s", [1, 128], F32)
        knwB_s = sb("knwB_s", [1, 128], F32)

        pb = [ctx.enter_context(nc.psum_tensor("pb%d" % i, [128, 512], F32))
              for i in range(8)]

        sems = {s: ctx.enter_context(nc.semaphore(s)) for s in SEMS}

        o_epsrms, o_epsln = 0, 1
        o_mean, o_var, o_lnv, o_rstd = 2, 4, 6, 8
        o_sum0, o_sum1, o_ssq0, o_ssq1, o_m2 = 10, 12, 14, 16, 18
        o_rstdA, o_rstdB = 32, 544
        o_sstat = 1056
        o_maxes = 1568
        o_maxhead = 1824
        o_craw, o_biasv = 1856, 1872
        o_biasT = 1888
        o_recipA, o_recipB = 1904, 1920
        o_recipA, o_recipB = o_rstdA, o_rstdB

        FA_x0, FA_x1 = 0, 1024
        FA_xn = (2048, 3072)
        FA_sq = 4096
        FA_sq2 = 5120
        FA_bc = 5632
        FA_att = 6144
        FA_x2 = 7168

        B3_W = 0
        B3_vaug = 2048
        B3_wcs = 10304
        B3_qf = 12352
        B3_ot = 13376
        B3_y0 = 12352
        B3_railsY = 2048
        B3_railsZ = 6400
        B3_wo = 2048
        B3_oTt = 0
        B3_wcb = 0
        B3_hg = 2048

        B1_qT = (0, 4096)
        B1_kT = (8192, 12288)
        B1_vau = (16384, 20544)

        def mm(dst, lhsT, rhs, start=True, stop=True, tile=None, deps=()):
            return S("tensor",
                     lambda t, d=dst, l=lhsT, r=rhs, st=start, sp=stop,
                     tp=tile: t.matmul(d, l, r, start=st, stop=sp,
                                       tile_position=tp),
                     deps=deps)

        def tr(dst, src, deps=()):
            return S("tensor",
                     lambda t, d=dst, s=src: t.transpose(d, s, eye_s[:]),
                     deps=deps)

        def act(out, in_, func, bias=None, scale=None, accum=None, deps=(),
                tiny=False):
            kw = {}
            if bias is not None:
                kw["bias"] = bias
            if scale is not None:
                kw["scale"] = scale
            if accum is not None:
                kw["accum_out"] = accum
            return S("scalar",
                     lambda s, o=out, i=in_, f=func, k=kw:
                     s.activation(o, i, f, **k),
                     deps=deps, tiny=tiny, pre_drain=tiny)

        def acopy(out, in_, deps=(), tiny=False):
            return S("scalar", lambda s, o=out, i=in_: s.copy(o, i),
                     deps=deps, tiny=tiny, pre_drain=tiny)

        def dve(fn, deps=(), tiny=False):
            return S("vector", fn, deps=deps, tiny=tiny, pre_drain=tiny)

        def dma(queue, dst, src, deps=()):
            return S(queue, lambda e, d=dst, s=src: e.dma_start(d, s),
                     deps=deps, dma=True)

        ld_eye = dma("gpsimd", eye_s[:], eye_d)
        ld_ones1 = dma("sync", ones1_s[:], ones1_d)
        ld_indA = dma("sync", indA_s[:], indA_d)
        ld_indB = dma("sync", indB_s[:], indB_d)
        ld_indAr = dma("sync", indAr_s[:], indAr_d)
        ld_indBr = dma("sync", indBr_s[:], indBr_d)
        ld_qnw = [dma("sync", qnwA_s[:], qnwA_d), dma("sync", qnwB_s[:], qnwB_d)]
        ld_knw = [dma("sync", knwA_s[:], knwA_d), dma("sync", knwB_s[:], knwB_d)]
        ms1 = S("vector", lambda v: v.memset(SM[:, o_epsrms:o_epsrms + 1],
                                             RMS_EPS), tiny=True)
        ms2 = S("vector", lambda v: v.memset(SM[:, o_epsln:o_epsln + 1],
                                             LN_EPS), tiny=True)
        vaug = BF3[:, B3_vaug:B3_vaug + 4160]
        vaug_ones = []
        for jc in range(32):
            for extra in (64, 129):
                vaug_ones.append(
                    S("gpsimd", lambda g, c=B3_vaug + jc * 130 + extra:
                      g.memset(BF3[:, c:c + 1], 1.0), tiny=True))

        xt_tile = [FA[:, FA_x0:FA_x0 + 1024], FA[:, FA_x1:FA_x1 + 1024]]

        def ln_stats(src0, src1, slot, deps0, deps1, from_psum=False):
            mean = SM[:, o_mean + slot:o_mean + slot + 1]
            var = SM[:, o_var + slot:o_var + slot + 1]
            lnv = SM[:, o_lnv + slot:o_lnv + slot + 1]
            rstd = SM[:, o_rstd + slot:o_rstd + slot + 1]
            s0 = SM[:, o_sum0 + slot:o_sum0 + slot + 1]
            s1 = SM[:, o_sum1 + slot:o_sum1 + slot + 1]
            q0 = SM[:, o_ssq0 + slot:o_ssq0 + slot + 1]
            q1 = SM[:, o_ssq1 + slot:o_ssq1 + slot + 1]
            m2 = SM[:, o_m2 + slot:o_m2 + slot + 1]
            sq = FA[:, FA_sq:FA_sq + 1024]
            readers = []
            if src1 is None:
                r1 = dve(lambda v, o=s0, i=src0:
                         v.tensor_reduce(o, i, axis=AX.X, op=ALU.add),
                         deps=deps0, tiny=True)
                a1 = act(sq[:, 0:1024], src0, AF.Square, accum=q0,
                         deps=deps0 + ([r1] if from_psum else []), tiny=True)
                c1, c2 = r1, a1
                readers = [r1, a1]
            else:
                r1 = dve(lambda v, o=s0, i=src0:
                         v.tensor_reduce(o, i, axis=AX.X, op=ALU.add),
                         deps=deps0, tiny=True)
                r2 = dve(lambda v, o=s1, i=src1:
                         v.tensor_reduce(o, i, axis=AX.X, op=ALU.add),
                         deps=deps1, tiny=True)
                a1 = act(sq[:, 0:512], src0, AF.Square, accum=q0,
                         deps=deps0 + ([r1] if from_psum else []), tiny=True)
                a2 = act(sq[:, 512:1024], src1, AF.Square, accum=q1,
                         deps=deps1 + ([r2] if from_psum else []), tiny=True)
                c1 = dve(lambda v, o=s0, a=s0, b=s1: v.tensor_add(o, a, b),
                         deps=[r1, r2], tiny=True)
                c2 = dve(lambda v, o=q0, a=q0, b=q1: v.tensor_add(o, a, b),
                         deps=[a1, a2], tiny=True)
                readers = [r1, r2, a1, a2]
            t1 = dve(lambda v, o=mean, i=s0:
                     v.tensor_scalar(o, i, 1.0 / D, None, ALU.mult),
                     deps=[c1], tiny=True)
            t2 = dve(lambda v, o=var, i=q0:
                     v.tensor_scalar(o, i, 1.0 / D, None, ALU.mult),
                     deps=[c2], tiny=True)
            t3 = dve(lambda v, o=m2, a=mean, b=mean: v.tensor_mul(o, a, b),
                     deps=[t1], tiny=True)
            t4 = dve(lambda v, o=var, a=var, b=m2: v.tensor_sub(o, a, b),
                     deps=[t2, t3], tiny=True)
            a3 = act(lnv, var, AF.Ln, bias=SM[:, o_epsln:o_epsln + 1],
                     deps=[t4, ms2], tiny=True)
            a4 = act(rstd, lnv, AF.Exp, scale=-0.5, deps=[a3], tiny=True)
            return mean, rstd, a4, readers

        tr_slot = {}
        p1_evac = []
        for st in range(NST):
            par = st % 2
            ld = dma("sync", xt_tile[par], x_d[st * 128:(st + 1) * 128, :],
                     deps=g_get(("xt", par)))
            mean, rstd, statop, readers = ln_stats(
                xt_tile[par][:, 0:1024], None, par, [ld], None)
            xn = FA[:, FA_xn[par]:FA_xn[par] + 1024]
            nrm = dve(lambda v, o=xn, i=xt_tile[par], m=mean, r=rstd:
                      v.tensor_scalar(o, i, m, r, ALU.subtract, ALU.mult),
                      deps=[statop] + g_get(("xn", par)))
            g_set(("xt", par), readers + [nrm])
            trs = []
            for c in range(8):
                slot = (st * 8 + c) % 2
                bank = pb[slot]
                t = tr(bank[:, 0:128], xn[:, c * 128:c * 128 + 128],
                       deps=[nrm, ld_eye, tr_slot.get(slot)])
                ev = acopy(BF1[:, c * 4096 + st * 128:
                               c * 4096 + st * 128 + 128],
                           bank[:, 0:128], deps=[t])
                tr_slot[slot] = ev
                trs.append(t)
                p1_evac.append(ev)
            g_set(("xn", par), trs)
        p1_done = p1_evac[-1]
        sched.phase_idx["p1"] = len(sched.ops)

        wt_tile = [BF3[:, B3_W:B3_W + 1024], BF3[:, B3_W + 1024:B3_W + 2048],
                   BF3[:, 10304:10304 + 1024], BF3[:, 11328:11328 + 1024]]
        wseq = [0]
        qf_tile = [BF3[:, B3_qf:B3_qf + 512], BF3[:, B3_qf + 512:B3_qf + 1024]]
        projb = (2, 3)
        statb = (4, 5)
        bcb = 6
        wsrc = {"q": wqT_d, "k": wkT_d, "v": wvT_d}
        whb = {"q": qT_h, "k": kT_h}

        def wslice_ap(wd, hp):
            return wd.rearrange("(kc p) c -> p kc c", p=128)[
                :, :, hp * 128:(hp + 1) * 128]

        def w3d(tile):
            return tile.rearrange("p (kc c) -> p kc c", kc=8)

        maxes_written = []
        last_vdma = None
        for hp in range(HP):
            for dsti, dst in enumerate(("q", "k")):
                wpar = wseq[0] % 4
                wseq[0] += 1
                ldw = dma("gpsimd", w3d(wt_tile[wpar]),
                          wslice_ap(wsrc[dst], hp),
                          deps=g_get(("wt", wpar)))
                wreaders = []
                for ib in range(NB):
                    ppar = ib % 2
                    bank = pb[projb[ppar]]
                    mmop = None
                    for kc in range(8):
                        mmop = mm(bank[:, :],
                                  wt_tile[wpar][:, kc * 128:kc * 128 + 128],
                                  BF1[:, kc * 4096 + ib * 512:
                                      kc * 4096 + ib * 512 + 512],
                                  start=(kc == 0), stop=(kc == 7),
                                  deps=([ldw, p1_done] +
                                        g_get(("projb", ppar))
                                        if kc == 0 else ()))
                    wreaders.append(mmop)
                    sq = FA[:, FA_sq:FA_sq + 512]
                    sqop = act(sq, bank[:, :], AF.Square,
                               deps=[mmop] + g_get("sqr"))
                    stA = mm(pb[statb[0]][0:1, :], indA_s[:], sq,
                             deps=[sqop, ld_indA] + g_get(("stb", 0)))
                    stB = mm(pb[statb[1]][0:1, :], indB_s[:], sq,
                             deps=[sqop, ld_indB] + g_get(("stb", 1)))
                    g_set("sqr", [stA, stB])
                    rstdA = SM[0:1, o_rstdA:o_rstdA + 512]
                    rstdB = SM[0:1, o_rstdB:o_rstdB + 512]
                    lA = act(rstdA, pb[statb[0]][0:1, :], AF.Ln,
                             bias=SM[0:1, o_epsrms:o_epsrms + 1],
                             scale=1.0 / DH, deps=[stA, ms1])
                    eA = act(rstdA, rstdA, AF.Exp, scale=-0.5, deps=[lA],
                             tiny=True)
                    lB = act(rstdB, pb[statb[1]][0:1, :], AF.Ln,
                             bias=SM[0:1, o_epsrms:o_epsrms + 1],
                             scale=1.0 / DH, deps=[stB, ms1])
                    eB = act(rstdB, rstdB, AF.Exp, scale=-0.5, deps=[lB],
                             tiny=True)
                    g_set(("stb", 0), [lA])
                    g_set(("stb", 1), [lB])
                    wA = qnwA_s if dst == "q" else knwA_s
                    wB = qnwB_s if dst == "q" else knwB_s
                    bc1 = mm(pb[bcb][:, :], wA[:], rstdA, start=True,
                             stop=False,
                             deps=[eA] + ld_qnw + ld_knw + g_get("bcb"))
                    bc2 = mm(pb[bcb][:, :], wB[:], rstdB, start=False,
                             stop=True, deps=[eB])
                    bce = act(FA[:, FA_bc:FA_bc + 512], pb[bcb][:, :],
                              AF.Identity, deps=[bc2] + g_get("fabc"))
                    g_set("bcb", [bce])
                    qfin = qf_tile[ppar]
                    qop = dve(lambda v, o=qfin, a=bank[:, :],
                              b=FA[:, FA_bc:FA_bc + 512]:
                              v.tensor_tensor(o, a, b, op=ALU.mult),
                              deps=[bce] + g_get(("qf", ppar)))
                    guards["fabc"] = [qop]
                    g_set(("projb", ppar), [qop, sqop])
                    st_dma = dma("sync",
                                 whb[dst][hp * 128:(hp + 1) * 128,
                                          ib * 512:(ib + 1) * 512],
                                 qfin, deps=[qop])
                    sq2 = FA[:, FA_sq2:FA_sq2 + 512]
                    sqop2 = act(sq2, qfin, AF.Square,
                                deps=[qop] + g_get("sq2r"))
                    g_set(("qf", ppar), [st_dma, sqop2])
                    fsA = mm(pb[statb[0]][0:1, :], indA_s[:], sq2,
                             deps=[sqop2] + g_get(("stb", 0)))
                    fsB = mm(pb[statb[1]][0:1, :], indB_s[:], sq2,
                             deps=[sqop2] + g_get(("stb", 1)))
                    g_set("sq2r", [fsA, fsB])
                    for hd, fs, bk in ((0, fsA, statb[0]), (1, fsB, statb[1])):
                        slot = (dsti * 16 + hp * 2 + hd) * 8 + ib
                        mx = dve(lambda v,
                                 o=SM[0:1, o_maxes + slot:o_maxes + slot + 1],
                                 i=pb[bk][0:1, :]:
                                 v.tensor_reduce(o, i, axis=AX.X, op=ALU.max),
                                 deps=[fs], tiny=True)
                        maxes_written.append(mx)
                        g_set(("stb", hd), [mx])
                g_set(("wt", wpar), wreaders)

            wpar = wseq[0] % 4
            wseq[0] += 1
            ldw = dma("gpsimd", w3d(wt_tile[wpar]), wslice_ap(wvT_d, hp),
                      deps=g_get(("wt", wpar)))
            wreaders = []
            vops = []
            for ib in range(NB):
                ppar = ib % 2
                bank = pb[projb[ppar]]
                mmop = None
                for kc in range(8):
                    mmop = mm(bank[:, :],
                              wt_tile[wpar][:, kc * 128:kc * 128 + 128],
                              BF1[:, kc * 4096 + ib * 512:
                                  kc * 4096 + ib * 512 + 512],
                              start=(kc == 0), stop=(kc == 7),
                              deps=([ldw, p1_done] + g_get(("projb", ppar))
                                    if kc == 0 else ()))
                wreaders.append(mmop)
                vt = FA[:, FA_att:FA_att + 512]
                ev = acopy(vt, bank[:, :], deps=[mmop] + g_get("vt"))
                g_set(("projb", ppar), [ev])
                trs = []
                for c in range(4):
                    jc = ib * 4 + c
                    slot = jc % 2
                    bank2 = pb[slot]
                    t = tr(bank2[:, 0:128],
                           vt[:, c * 128:c * 128 + 128],
                           deps=[ev, ld_eye, tr_slot.get(slot)] +
                                g_get("vaug_dma"))
                    e1 = acopy(vaug[:, jc * 130:jc * 130 + 64],
                               bank2[:, 0:64], deps=[t])
                    e2 = acopy(vaug[:, jc * 130 + 65:jc * 130 + 129],
                               bank2[:, 64:128], deps=[t])
                    tr_slot[slot] = e2
                    trs.append(t)
                    vops.extend([e1, e2])
                g_set("vt", trs)
            vd = dma("sync", v_h[hp * 128:(hp + 1) * 128, :], vaug,
                     deps=vops + vaug_ones)
            g_set("vaug_dma", [vd])
            last_vdma = vd
            g_set(("wt", wpar), wreaders + [vd])

        mh_ops = []
        for i in range(32):
            op = dve(lambda v, o=SM[0:1, o_maxhead + i:o_maxhead + i + 1],
                     s=SM[0:1, o_maxes + i * 8:o_maxes + i * 8 + 8]:
                     v.tensor_reduce(o, s, axis=AX.X, op=ALU.max),
                     deps=[], tiny=True)
            mh_ops.append(op)
        craw = SM[0:1, o_craw:o_craw + 16]
        cop = dve(lambda v, o=craw, a=SM[0:1, o_maxhead:o_maxhead + 16],
                  b=SM[0:1, o_maxhead + 16:o_maxhead + 32]:
                  v.tensor_mul(o, a, b), deps=[], tiny=True)
        csq = act(craw, craw, AF.Sqrt, deps=[cop], tiny=True)
        biasv = SM[0:1, o_biasv:o_biasv + 16]
        bop = dve(lambda v, o=biasv, i=craw:
                  v.tensor_scalar(o, i, -SCALE, MARGIN, ALU.mult, ALU.add),
                  deps=[csq], tiny=True)
        bcm = mm(pb[bcb][:, 0:16], ones1_s[:], biasv,
                 deps=[bop, ld_ones1] + g_get("bcb"))
        biasT = SM[:, o_biasT:o_biasT + 16]
        bias_ready = acopy(biasT, pb[bcb][:, 0:16], deps=[bcm], tiny=True)
        sched.phase_idx["p15"] = len(sched.ops)


        qT_buf = [BF1[:, B1_qT[0]:B1_qT[0] + 4096],
                  BF1[:, B1_qT[1]:B1_qT[1] + 4096]]
        kT_buf = [BF1[:, B1_kT[0]:B1_kT[0] + 4096],
                  BF1[:, B1_kT[1]:B1_kT[1] + 4096]]
        va_buf = [BF1[:, B1_vau[0]:B1_vau[0] + 4160],
                  BF1[:, B1_vau[1]:B1_vau[1] + 4160]]
        Ablk = [BF2[:, 0:16384], BF2[:, 16384:32768]]
        ot_tile = [BF3[:, B3_ot:B3_ot + 512], BF3[:, B3_ot + 512:B3_ot + 1024]]
        SCB = (0, 1, 2)
        AVB = {0: (3, 4), 1: (5, 6)}
        RCB = 7
        for s in range(3):
            g_set(("scb3", s), [bias_ready])
        for hd in range(2):
            for p in range(2):
                g_set(("avb", hd, p), [bias_ready])
        g_set("rcb", [bias_ready])
        g_set(("qkv", 0), [last_vdma])
        g_set(("qkv", 1), [last_vdma])
        Ause = {}
        qkv_loaded = [None, None]

        def load_hp(hp):
            par = hp % 2
            dep = g_get(("qkv", par))
            l1 = dma("sync", qT_buf[par], qT_h[hp * 128:(hp + 1) * 128, :],
                     deps=dep)
            l2 = dma("sync", kT_buf[par], kT_h[hp * 128:(hp + 1) * 128, :],
                     deps=dep)
            l3 = dma("sync", va_buf[par], v_h[hp * 128:(hp + 1) * 128, :],
                     deps=dep)
            qkv_loaded[par] = [l1, l2, l3]

        g_set("p2bc", g_get("fabc"))
        last_od = [None, None]
        step = 0
        load_hp(0)
        for hp in range(HP):
            par = hp % 2
            if hp + 1 < HP:
                load_hp(hp + 1)
            loads = qkv_loaded[par]
            av_done = {}
            pend = []
            eop_of = {}

            def emit_tail(ib):
                apar = ib % 2
                otp = ib % 2
                rsb = [SM[0:1, o_recipA:o_recipA + 512],
                       SM[0:1, o_recipB:o_recipB + 512]]
                cc = []
                for hd in range(2):
                    cc.append(acopy(rsb[hd], pb[AVB[hd][apar]][64:65, :],
                                    deps=[av_done[(ib, hd)]] +
                                         g_get(("rrow", hd))))
                bc1 = mm(pb[RCB][:, :], indAr_s[:], rsb[0], start=True,
                         stop=False,
                         deps=[cc[0], ld_indAr, ld_indBr] + g_get("rcb"))
                bc2 = mm(pb[RCB][:, :], indBr_s[:], rsb[1], start=False,
                         stop=True, deps=[cc[1]])
                g_set(("rrow", 0), [bc1])
                g_set(("rrow", 1), [bc2])
                dbc = FA[:, FA_bc:FA_bc + 512]
                bce = act(dbc, pb[RCB][:, :], AF.Identity,
                          deps=[bc2] + g_get("p2bc"))
                g_set("rcb", [bce])
                rcop = dve(lambda v, o=dbc, i=dbc: v.reciprocal(o, i),
                           deps=[bce])
                oops = []
                for hd in range(2):
                    oop = dve(lambda v,
                              o=ot_tile[otp][hd * 64:hd * 64 + 64, :],
                              a=pb[AVB[hd][apar]][0:64, :],
                              b=dbc[hd * 64:hd * 64 + 64, :]:
                              v.tensor_tensor(o, a, b, op=ALU.mult),
                              deps=[rcop] + g_get(("ot", otp)))
                    g_set(("avb", hd, apar), [oop, cc[hd]])
                    oops.append(oop)
                g_set("p2bc", oops + [rcop])
                od = dma("sync", oT_h[hp * 128:(hp + 1) * 128,
                                      ib * 512:(ib + 1) * 512],
                         ot_tile[otp], deps=oops)
                g_set(("ot", otp), [od])
                last_od[otp] = od

            def issue_avop(ib, jc, hd):
                apar = ib % 2
                vcol = jc * 130 + hd * 65
                avop = mm(pb[AVB[hd][apar]][0:65, :],
                          va_buf[par][:, vcol:vcol + 65],
                          Ablk[hd][:, jc * 512:jc * 512 + 512],
                          start=(jc == 0), stop=(jc == 31),
                          deps=[eop_of[(ib, jc, hd)], loads[2]] +
                               (g_get(("avb", hd, apar)) if jc == 0 else []))
                Ause[(hd, jc)] = avop
                if jc == 31:
                    av_done[(ib, hd)] = avop
                    if hd == 1:
                        emit_tail(ib)
                return avop

            for ib in range(NB):
                for jc in range(32):
                    for hd in range(2):
                        s = step % 3
                        step += 1
                        sb_ = pb[SCB[s]]
                        prow = hd * 64
                        scop = mm(sb_[:, :],
                                  kT_buf[par][prow:prow + 64,
                                              jc * 128:jc * 128 + 128],
                                  qT_buf[par][prow:prow + 64,
                                              ib * 512:ib * 512 + 512],
                                  tile=(prow, 0),
                                  deps=[loads[0], loads[1]] +
                                       g_get(("scb3", s)))
                        adst = Ablk[hd][:, jc * 512:jc * 512 + 512]
                        eop = act(adst, sb_[:, :], AF.Exp,
                                  bias=biasT[:, hp * 2 + hd:hp * 2 + hd + 1],
                                  scale=SCALE,
                                  deps=[scop, bias_ready,
                                        Ause.get((hd, jc))])
                        g_set(("scb3", s), [eop])
                        eop_of[(ib, jc, hd)] = eop
                        pend.append((ib, jc, hd))
                        if len(pend) >= 6:
                            for _ in range(3):
                                issue_avop(*pend.pop(0))
            while pend:
                issue_avop(*pend.pop(0))
            g_set(("qkv", par), [av_done[(NB - 1, 0)], av_done[(NB - 1, 1)]])
        p2_last_od = [d for d in last_od if d is not None]
        g_set(("scb", 1, 0), p2_last_od)
        g_set(("scb", 1, 1), p2_last_od)
        sched.phase_idx["p2"] = len(sched.ops)

        oTt = [BF3[:, B3_oTt:B3_oTt + 1024],
               BF3[:, B3_oTt + 1024:B3_oTt + 2048]]
        wo_sb = BF3[:, B3_wo:B3_wo + 8192]
        ldwo = []
        for c in range(8):
            deps = []
            if c == 0:
                deps = (g_get(("qkv", 0)) + g_get(("qkv", 1)) +
                        p2_last_od + g_get(("ot", 0)) + g_get(("ot", 1)) +
                        g_get(("wcs", 0)) + g_get(("wcs", 1)) +
                        g_get(("wt", 0)) + g_get(("wt", 1)) +
                        g_get("vaug_dma"))
            ldwo.append(dma("gpsimd", wo_sb[:, c * 1024:(c + 1) * 1024],
                            woT_d[c * 128:(c + 1) * 128, :], deps=deps))
        wob = (0, 1)
        g_set(("wob", 0), [bias_ready] + p2_last_od)
        g_set(("wob", 1), [bias_ready] + p2_last_od)
        oT_src = oT_h.rearrange("(kc p) t -> p kc t", p=128)
        ln2_evac = []
        x2st_done = []
        tr3_slot = {}
        pend_trs = None
        for st in range(NST):
            par = st % 2
            lo = dma("sync", w3d(oTt[par]),
                     oT_src[:, :, st * 128:(st + 1) * 128],
                     deps=g_get(("oTt", par)) + p2_last_od +
                          (g_get(("wt", 0)) + g_get(("wt", 1))
                           if st < 2 else []))
            lx = dma("sync", xt_tile[par], x_d[st * 128:(st + 1) * 128, :],
                     deps=g_get(("xt", par)))
            mops = []
            for nh in range(2):
                mmop = None
                for kc in range(8):
                    mmop = mm(pb[wob[nh]][:, :],
                              oTt[par][:, kc * 128:kc * 128 + 128],
                              wo_sb[:, kc * 1024 + nh * 512:
                                    kc * 1024 + nh * 512 + 512],
                              start=(kc == 0), stop=(kc == 7),
                              deps=([ldwo[kc], lo] + g_get(("wob", nh))
                                    if kc == 0 else [ldwo[kc]]))
                mops.append(mmop)
            g_set(("oTt", par), mops)
            attb = (FA[:, FA_att:FA_att + 1024] if par == 0
                    else FA[:, FA_sq2:FA_sq2 + 1024])
            e0 = acopy(attb[:, 0:512], pb[wob[0]][:, :],
                       deps=[mops[0]] + g_get(("p3evt", par)))
            e1 = acopy(attb[:, 512:1024], pb[wob[1]][:, :],
                       deps=[mops[1]] + g_get(("p3evt", par)))
            g_set(("wob", 0), [e0])
            g_set(("wob", 1), [e1])
            mean, rstd, statop, readers = ln_stats(
                attb[:, 0:1024], None, par, [e0, e1], None)
            nrm = dve(lambda v, o=attb, i=attb, m=mean, r=rstd:
                      v.tensor_scalar(o, i, m, r, ALU.subtract, ALU.mult),
                      deps=[statop])
            x2t = FA[:, FA_x2:FA_x2 + 1024]
            xa = dve(lambda v, o=x2t, a=xt_tile[par], b=attb:
                     v.tensor_add(o, a, b),
                     deps=[lx, nrm] + g_get("x2t"))
            g_set(("p3evt", par), readers + [nrm, xa])
            g_set(("xt", par), [xa])
            sx2 = dma("scalar", x2_h[st * 128:(st + 1) * 128, :], x2t,
                      deps=[xa])
            x2st_done.append(sx2)
            mean2, rstd2, statop2, readers2 = ln_stats(
                x2t[:, 0:1024], None, par, [xa], None)
            ln2f = FA[:, FA_xn[par]:FA_xn[par] + 1024]
            l2 = dve(lambda v, o=ln2f, i=x2t, m=mean2, r=rstd2:
                     v.tensor_scalar(o, i, m, r, ALU.subtract, ALU.mult),
                     deps=[statop2] + g_get(("xn", par)))
            g_set("x2t", [sx2] + readers2 + [l2])

            def emit_trs(st_, l2_, ln2f_):
                par_ = st_ % 2
                trs = []
                for c in range(8):
                    slot = (st_ * 8 + c) % 2
                    bank = pb[slot + 2]
                    prior = tr3_slot.get(slot)
                    extra = ([] if prior is not None else
                             g_get(("scb", 1, 0)) + g_get(("scb", 1, 1)))
                    t = tr(bank[:, 0:128], ln2f_[:, c * 128:c * 128 + 128],
                           deps=[l2_, prior] + extra)
                    ev = acopy(BF1[:, c * 4096 + st_ * 128:
                                   c * 4096 + st_ * 128 + 128],
                               bank[:, 0:128], deps=[t])
                    tr3_slot[slot] = ev
                    trs.append(t)
                    ln2_evac.append(ev)
                g_set(("xn", par_), trs)

            if pend_trs is not None:
                emit_trs(*pend_trs)
            pend_trs = (st, l2, ln2f)
        emit_trs(*pend_trs)
        p3_done = ln2_evac[-1]
        sched.phase_idx["p3"] = len(sched.ops)

        railsY_sb = BF3[:, B3_railsY:B3_railsY + 4096]
        ldry = dma("gpsimd", railsY_sb, railsY_d, deps=[p3_done, ldwo[-1]])
        rz_tile = [BF3[:, B3_railsZ:B3_railsZ + 1024],
                   BF3[:, B3_railsZ + 1024:B3_railsZ + 2048]]
        y_tile = [BF3[:, B3_y0:B3_y0 + 512], BF3[:, B3_y0 + 512:B3_y0 + 1024]]
        yb = (0, 1)
        zb = (2, 3)
        g_set(("yb", 0), [p3_done]); g_set(("yb", 1), [p3_done])
        g_set(("zb", 0), [p3_done]); g_set(("zb", 1), [p3_done])
        g_set(("ytl", 0), [p3_done]); g_set(("ytl", 1), [p3_done])
        g_set(("qfdead", 0), g_get(("qf", 0)))
        hstore = {}
        slab_last = None
        for rb in range(16):
            hh, qr = rb // 4, rb % 4
            rpar = rb % 2
            lz = dma("gpsimd", rz_tile[rpar],
                     railsZ_d[:, rb * 1024:(rb + 1) * 1024],
                     deps=g_get(("rz", rpar)) + [p3_done, ldwo[-1]])
            gel_last = [None] * 8
            zreads = {0: [], 1: []}
            for tb4 in range(NB):
                yev = []
                for kr in range(2):
                    kcx = 2 * qr + kr
                    ym = mm(pb[yb[kr]][:, :],
                            railsY_sb[:, kcx * 512 + hh * 128:
                                      kcx * 512 + hh * 128 + 128],
                            BF1[:, kcx * 4096 + tb4 * 512:
                                kcx * 4096 + tb4 * 512 + 512],
                            deps=[ldry, p3_done] + g_get(("yb", kr)))
                    ye = acopy(y_tile[kr], pb[yb[kr]][:, :],
                               deps=[ym] + g_get(("ytl", kr)) +
                                    (g_get(("qf", 0)) + g_get(("qf", 1))
                                     if rb == 0 and tb4 == 0 else []))
                    g_set(("yb", kr), [ye])
                    yev.append(ye)
                zr = {0: [], 1: []}
                for fc in range(8):
                    kr, m = fc // 4, fc % 4
                    zpar = fc % 2
                    zm = mm(pb[zb[zpar]][:, :],
                            rz_tile[rpar][:, fc * 128:fc * 128 + 128],
                            y_tile[kr],
                            deps=[lz, yev[kr]] + g_get(("zb", zpar)))
                    zr[kr].append(zm)
                    gl = act(BF2[:, fc * 4096 + tb4 * 512:
                                 fc * 4096 + tb4 * 512 + 512],
                             pb[zb[zpar]][:, :], AF.Gelu,
                             deps=[zm] + g_get(("slab", fc)) +
                                  ([Ause.get((fc // 4, 31))]
                                   if rb == 0 and tb4 == 0 else []))
                    g_set(("zb", zpar), [gl])
                    gel_last[fc] = gl
                for kr in range(2):
                    g_set(("ytl", kr), zr[kr])
                    zreads[kr].extend(zr[kr])
            sds = []
            for fc in range(8):
                fkc = hh * 32 + qr * 8 + fc
                sd = dma("sync", h_h[fkc * 128:(fkc + 1) * 128, :],
                         BF2[:, fc * 4096:(fc + 1) * 4096],
                         deps=[gel_last[fc]])
                g_set(("slab", fc), [sd])
                hstore[fkc] = sd
                sds.append(sd)
            g_set(("rz", rpar), zreads[0] + zreads[1])
            slab_last = sds[-1]

        sched.phase_idx["p4a"] = len(sched.ops)
        wcb_tile = [BF3[:, B3_wcb:B3_wcb + 1024],
                    BF3[:, B3_wcb + 1024:B3_wcb + 2048],
                    BF3[:, 8448:8448 + 1024],
                    BF3[:, 9472:9472 + 1024]]
        hg_tile = [BF3[:, B3_hg:B3_hg + 2048],
                   BF3[:, B3_hg + 2048:B3_hg + 4096]]
        h_src = h_h.rearrange("(kg g p) t -> kg p g t", p=128, g=4)
        for b in range(8):
            g_set(("accb", b), [slab_last])
        g_set(("hg", 0), [slab_last]); g_set(("hg", 1), [slab_last])
        for w in range(4):
            g_set(("wcbt", w), [slab_last])
        y_dmas = []
        for stg in range(8):
            acc_stops = []
            for kg in range(32):
                gpar = kg % 2
                lh = dma("sync",
                         hg_tile[gpar].rearrange("p (g t) -> p g t", g=4),
                         h_src[kg, :, :, stg * 512:(stg + 1) * 512],
                         deps=g_get(("hg", gpar)) +
                              [hstore[kg * 4 + g] for g in range(4)])
                hg_readers = []
                for kcl in range(4):
                    kc = kg * 4 + kcl
                    wpar = kc % 4
                    lw = dma("gpsimd", wcb_tile[wpar],
                             wcT_d[kc * 128:(kc + 1) * 128, :],
                             deps=g_get(("wcbt", wpar)))
                    wreaders = []
                    for st4 in range(4):
                        for nh in range(2):
                            bnk = st4 * 2 + nh
                            op = mm(pb[bnk][:, :],
                                    hg_tile[gpar][:,
                                                  kcl * 512 + st4 * 128:
                                                  kcl * 512 + st4 * 128 + 128],
                                    wcb_tile[wpar][:, nh * 512:nh * 512 + 512],
                                    start=(kc == 0), stop=(kc == 127),
                                    deps=[lh, lw] +
                                         (g_get(("accb", bnk))
                                          if kc == 0 else []))
                            wreaders.append(op)
                            hg_readers.append(op)
                            if kc == 127:
                                acc_stops.append(op)
                    g_set(("wcbt", wpar), wreaders)
                g_set(("hg", gpar), hg_readers)
            for st4 in range(4):
                st = stg * 4 + st4
                par = st % 2
                m0, m1 = acc_stops[st4 * 2], acc_stops[st4 * 2 + 1]
                evt = FA[:, FA_xn[par]:FA_xn[par] + 1024]
                e0 = acopy(evt[:, 0:512], pb[st4 * 2][:, :],
                           deps=[m0] + g_get(("evt", par)))
                e1 = acopy(evt[:, 512:1024], pb[st4 * 2 + 1][:, :],
                           deps=[m1] + g_get(("evt", par)))
                g_set(("accb", st4 * 2), [e0])
                g_set(("accb", st4 * 2 + 1), [e1])
                lx2 = dma("scalar", xt_tile[par],
                          x2_h[st * 128:(st + 1) * 128, :],
                          deps=g_get(("xt", par)) + [x2st_done[st]])
                mean, rstd, statop, readers = ln_stats(
                    evt[:, 0:1024], None, par, [e0, e1], None)
                att = FA[:, FA_att:FA_att + 1024]
                n0 = dve(lambda v, o=att, i=evt, m=mean, r=rstd:
                         v.tensor_scalar(o, i, m, r, ALU.subtract, ALU.mult),
                         deps=[statop] + g_get("p4att"))
                g_set(("evt", par), readers + [n0])
                yt = FA[:, FA_x2:FA_x2 + 1024]
                ya = dve(lambda v, o=yt, a=xt_tile[par], b=att:
                         v.tensor_add(o, a, b),
                         deps=[lx2, n0] + g_get("x2t"))
                g_set("p4att", [ya])
                g_set(("xt", par), [ya])
                yd = dma("scalar", y_d[st * 128:(st + 1) * 128, :], yt,
                         deps=[ya])
                g_set("x2t", [yd])
                y_dmas.append(yd)

        S("sync", lambda e: e.nop(), deps=y_dmas)
        sched.phase_idx["end"] = len(sched.ops)

        sched.finalize(max_ops)
        block = ctx.enter_context(nc.Block())
        block.tensor(lambda t: sched.emit_engine("tensor", t, sems))
        block.scalar(lambda s: sched.emit_engine("scalar", s, sems))
        block.vector(lambda v: sched.emit_engine("vector", v, sems))
        block.gpsimd(lambda g: sched.emit_engine("gpsimd", g, sems))
        block.sync(lambda e: sched.emit_engine("sync", e, sems))

    nc._dev_phase_idx = dict(sched.phase_idx)
    return nc


_CACHED = {}


def _get_program():
    if "nc" not in _CACHED:
        _CACHED["nc"] = build_program()
    return _CACHED["nc"]


def _pack_inputs(inputs):
    f = np.float32
    Wq = np.asarray(inputs["L0_attn_Wq"], f)
    Wk = np.asarray(inputs["L0_attn_Wk"], f)
    Wv = np.asarray(inputs["L0_attn_Wv"], f)
    Wo = np.asarray(inputs["L0_attn_Wo"], f)
    qn = np.asarray(inputs["L0_attn_qn_w"], f)
    kn = np.asarray(inputs["L0_attn_kn_w"], f)
    rm = np.asarray(inputs["L0_ffn_rail_maker"], f)
    ru = np.asarray(inputs["L0_ffn_rail_up_proj"], f)
    Wc = np.asarray(inputs["L0_ffn_w_cproj"], f)

    common = {
        "wqT": np.ascontiguousarray(Wq.T),
        "wkT": np.ascontiguousarray(Wk.T),
        "wvT": np.ascontiguousarray(Wv.T),
        "woT": np.ascontiguousarray(Wo.T),
        "wcT": np.ascontiguousarray(Wc.T),
    }
    Y = np.zeros((8, 4, 128, 128), f)
    for kc in range(8):
        for hh in range(4):
            for rr in range(8):
                Y[kc, hh, 16 * rr:16 * rr + 16, 16 * rr:16 * rr + 16] = \
                    rm[hh, 8 * kc + rr]
    common["railsY"] = np.ascontiguousarray(
        Y.transpose(2, 0, 1, 3).reshape(128, 4096))
    Z = np.zeros((8, 4, 4, 128, 128), f)
    for kc in range(8):
        for hh in range(4):
            for m in range(4):
                for rp in range(2):
                    r = 8 * kc + 2 * m + rp
                    Z[kc, hh, m, 16 * (2 * m + rp):16 * (2 * m + rp) + 16,
                      64 * rp:64 * rp + 64] = ru[hh, r].T
    Z2 = Z.reshape(4, 2, 4, 4, 128, 128)
    common["railsZ"] = np.ascontiguousarray(
        Z2.transpose(4, 2, 0, 1, 3, 5).reshape(128, 16384))
    za = np.zeros((1, 128), f); za[0, :64] = qn
    zbv = np.zeros((1, 128), f); zbv[0, 64:] = qn
    common["qnwA"], common["qnwB"] = za, zbv
    za = np.zeros((1, 128), f); za[0, :64] = kn
    zbv = np.zeros((1, 128), f); zbv[0, 64:] = kn
    common["knwA"], common["knwB"] = za, zbv
    ia = np.zeros((128, 1), f); ia[:64] = 1.0
    ib = np.zeros((128, 1), f); ib[64:] = 1.0
    common["indA"], common["indB"] = ia, ib
    iar = np.zeros((1, 128), f); iar[0, :64] = 1.0
    ibr = np.zeros((1, 128), f); ibr[0, 64:] = 1.0
    common["indAr"], common["indBr"] = iar, ibr
    common["ones1"] = np.ones((1, 128), f)
    common["eye"] = np.eye(128, dtype=f)
    return common


def kernel(**inputs):
    x = np.asarray(inputs["L0_attn_x"], np.float32)
    common = _pack_inputs(inputs)
    nc = _get_program()
    in_maps = []
    for c in range(NCORES):
        m = dict(common)
        m["x"] = np.ascontiguousarray(x[c])
        in_maps.append(m)
    res = run_bass_kernel_spmd(nc, in_maps, core_ids=list(range(NCORES)))
    return np.stack([np.asarray(r["y"], np.float32) for r in res.results], 0)
